# version 02
# baseline (speedup 1.0000x reference)
import numpy as np
import ml_dtypes
from contextlib import ExitStack

import concourse.bass as bass
import concourse.mybir as mybir
from concourse.bass_utils import run_bass_kernel_spmd

F32 = mybir.dt.float32
BF16 = mybir.dt.bfloat16
AF = mybir.ActivationFunctionType
ALU = mybir.AluOpType
AX = mybir.AxisListType.X

G = 8
T = 2048
TC = 512
NT = 4
CH = 128
NCH = 16
HP = 8
H, DH = 16, 64
DFF = 4096
SLAB = 512
NSLAB = DFF // SLAB
EPS = 1e-5
NTOK_GN = float(H * DH * T)

NSUB = 4
DBG_STOP = None
DBG_XN = False
DBG_DUMP = None


class Ctr:
    def __init__(self, sem):
        self.sem = sem
        self.v = 0

    def inc(self, inst, n=1):
        inst.then_inc(self.sem, n)
        self.v += n
        return self.v


def host_consts():
    gamma = (1.0 - 2.0 ** (-5.0 - np.arange(H, dtype=np.float32))).astype(np.float32)
    j = np.arange(CH, dtype=np.float32)
    diff = j[:, None] - j[None, :]
    dmask = np.where(diff[None] >= 0, gamma[:, None, None] ** np.maximum(diff[None], 0.0), 0.0).astype(np.float32)
    dmaskT = np.ascontiguousarray(np.transpose(dmask, (0, 2, 1)))
    gq = np.zeros((HP, 128, CH), np.float32)
    gc = np.zeros((HP, 128, CH), np.float32)
    gk = np.zeros((HP, 128, CH), np.float32)
    gcu = np.zeros((128, HP), np.float32)
    for hp in range(HP):
        for half in range(2):
            g_ = gamma[2 * hp + half]
            rows = slice(64 * half, 64 * half + 64)
            gq[hp, rows, :] = g_ ** j[None, :]
            gc[hp, rows, :] = g_ ** (CH - j[None, :])
            gk[hp, :, rows] = (g_ ** (CH - j[:, None]))
            gcu[rows, hp] = g_ ** CH
    sel = np.zeros((G, H, 128), np.float32)
    for g in range(G):
        for p in range(128):
            sel[g, (128 * g + p) // DH, p] = 1.0
    ident = np.eye(128, dtype=np.float32)
    ones32 = np.ones((1, TC), np.float32)
    return dict(dmask=dmask, dmaskT=dmaskT, gq=gq, gc=gc, gk=gk, gcu=gcu,
                sel=sel, ident=ident, ones32=ones32)


def build_nc(nsub=4):
    nc = bass.Bass("TRN2", target_bir_lowering=False)

    def din(name, shape):
        return nc.dram_tensor(name, list(shape), F32, kind="ExternalInput").ap()

    x_d = din("x", [1024, T])
    wq_d = din("wq", [1024, 1024])
    wk_d = din("wk", [1024, 1024])
    wv_d = din("wv", [1024, 1024])
    pkv_d = din("pkvT", [128, 1024])
    gnw_d = din("gnw", [H, 1])
    gnb_d = din("gnb", [H, 1])
    wwT_d = din("wwT", [1024, 3072])
    wb_d = din("wb", [1, 3072])
    owT_d = din("owT", [1024, 1024])
    ob_d = din("ob", [1, 1024])
    wr_d = din("wr", [1024, 8])
    wu_d = din("wu", [7, 1024, DFF])
    wg_d = din("wg", [7, 1024, DFF])
    wd_d = din("wd", [7, DFF, 1024])
    wus_d = din("wus", [1024, DFF])
    wgs_d = din("wgs", [1024, DFF])
    wds_d = din("wds", [DFF, 1024])
    dmask_d = din("dmask", [H, CH, CH])
    dmaskT_d = din("dmaskT", [H, CH, CH])
    gq_d = din("gq", [HP, 128, CH])
    gc_d = din("gc", [HP, 128, CH])
    gk_d = din("gk", [HP, 128, CH])
    gcu_d = din("gcu", [128, HP])
    sel_d = din("sel", [G, H, 128])
    ident_d = din("ident", [128, 128])
    ones32_d = din("ones32", [1, TC])
    y_d = nc.dram_tensor("y", [1024, T], F32, kind="ExternalOutput").ap()

    with ExitStack() as ctx:
        sb = lambda n, s, d: ctx.enter_context(nc.sbuf_tensor("sb_" + n, s, d))
        psm = lambda n, s, d: ctx.enter_context(nc.psum_tensor(n, s, d))
        sem = lambda n: ctx.enter_context(nc.semaphore(n))

        XB = sb("XB", [128, G, T], F32)
        XN = sb("XN", [128, G, T], BF16)
        FM = sb("FM", [128, G, T], BF16)
        id32 = sb("id32", [128, 128], F32)
        ones32 = sb("ones32", [1, TC], F32)
        onesbf = sb("onesbf", [1, TC], BF16)
        onescol = sb("onescol", [128, 1], BF16)
        onescol32 = sb("onescol32", [128, 1], F32)
        arow = sb("arow", [1, T], F32)
        brow = sb("brow", [1, T], F32)
        scrA = sb("scrA", [128, T], F32)
        vals = sb("vals", [1, 16], F32)
        gnw16 = sb("gnw16", [H, 1], F32)
        gnb16 = sb("gnb16", [H, 1], F32)
        rb16 = sb("rb16", [H, 2], F32)
        stat2 = sb("stat2", [H, 2], F32)
        stcol = sb("stcol", [128, 2 * G], F32)
        wr32 = sb("wr32", [128, 8, 8], F32)
        wrsrow = sb("wrsrow", [1, 8], F32)
        wrsbc = sb("wrsbc", [128, 8], F32)
        epsc = sb("epsc", [1, 1], F32)

        P1 = psm("P1", [128, T], F32)
        P2 = psm("P2", [128, T], F32)

        ld = Ctr(sem("s_ld"))
        gw = Ctr(sem("s_gw"))
        pe = Ctr(sem("s_pe"))
        ve = Ctr(sem("s_ve"))
        se = Ctr(sem("s_se"))
        gp = Ctr(sem("s_gp"))
        st = Ctr(sem("s_st"))
        ALLC = dict(ld=ld, gw=gw, pe=pe, ve=ve, se=se, gp=gp, st=st)

        block = ctx.enter_context(nc.Block())

        phase_id = [0]

        def snap():
            return {k: c.v for k, c in ALLC.items()}

        def wsnap(eng, s):
            for k, v in s.items():
                if v > 0:
                    eng.wait_ge(ALLC[k].sem, v)

        def f_sync0(sy):
            ld.inc(sy.dma_start(out=XB[:], in_=x_d.rearrange("(g p) t -> p g t", p=128)), 16)
            ld.inc(sy.dma_start(out=id32[:], in_=ident_d), 16)
            ld.inc(sy.dma_start(out=ones32[:], in_=ones32_d), 16)
            ld.inc(sy.dma_start(out=gnw16[:], in_=gnw_d), 16)
            ld.inc(sy.dma_start(out=gnb16[:], in_=gnb_d), 16)
            ld.inc(sy.dma_start(out=wr32[:], in_=wr_d.rearrange("(k p) e -> p k e", p=128)), 16)
        block.sync(f_sync0)

        def f_g0(g_):
            gw.inc(g_.dma_start(out=onesbf[:], in_=ones32_d), 16)
            gp.inc(g_.memset(onescol[:], 1.0), 1)
            gp.inc(g_.memset(onescol32[:], 1.0), 1)
            gp.inc(g_.memset(epsc[:], EPS), 1)
        block.gpsimd(f_g0)

        def emit_ln():
            s0 = snap()
            phase_id[0] += 1
            pid = str(phase_id[0])
            with nc.sbuf_tensor("ln_sum" + pid, [1, T], F32) as sumrow, \
                 nc.sbuf_tensor("ln_sq" + pid, [1, T], F32) as sqrow, \
                 nc.sbuf_tensor("ln_tmp" + pid, [1, T], F32) as tmprow:

                def f_s(se_):
                    wsnap(se_, s0)
                    for g in range(G):
                        i = se_.activation(XN[:, g, :], XB[:, g, :], AF.Copy)
                    for g in range(G):
                        i = se_.activation(FM[:, g, :], XB[:, g, :], AF.Square)
                    se.inc(i)
                block.scalar(f_s)
                se_ready = se.v

                def f_t(te):
                    wsnap(te, s0)
                    te.wait_ge(se.sem, se_ready)
                    for t in range(NT):
                        for g in range(G):
                            i = te.matmul(P1[0:1, t * TC:(t + 1) * TC], onescol[:],
                                          XN[:, g, t * TC:(t + 1) * TC], start=(g == 0), stop=(g == G - 1))
                        pe.inc(i)
                        for g in range(G):
                            i = te.matmul(P2[0:1, t * TC:(t + 1) * TC], onescol[:],
                                          FM[:, g, t * TC:(t + 1) * TC], start=(g == 0), stop=(g == G - 1))
                        pe.inc(i)
                block.tensor(f_t)
                pe_stats = pe.v

                def f_v(v):
                    v.wait_ge(pe.sem, pe_stats)
                    ve.inc(v.tensor_scalar(sumrow[:], P1[0:1, :], 1.0 / 1024.0, None, ALU.mult))
                    ve.inc(v.tensor_scalar(sqrow[:], P2[0:1, :], 1.0 / 1024.0, None, ALU.mult))
                    v.wait_ge(ve.sem, ve.v)
                    ve.inc(v.tensor_tensor(tmprow[:], sumrow[:], sumrow[:], ALU.mult))
                    v.wait_ge(ve.sem, ve.v)
                    ve.inc(v.tensor_tensor(sqrow[:], sqrow[:], tmprow[:], ALU.subtract))
                block.vector(f_v)
                ve_var = ve.v

                def f_s2(se_):
                    se_.wait_ge(ve.sem, ve_var)
                    se.inc(se_.activation(tmprow[:], sqrow[:], AF.Sqrt, bias=epsc[0:1, 0:1]))
                block.scalar(f_s2)
                se_sqrt = se.v

                def f_v2(v):
                    v.wait_ge(se.sem, se_sqrt)
                    ve.inc(v.reciprocal(arow[:], tmprow[:]))
                    v.wait_ge(ve.sem, ve.v)
                    ve.inc(v.scalar_tensor_tensor(brow[:], sumrow[:], -1.0, arow[:], ALU.mult, ALU.mult))
                block.vector(f_v2)
                ve_ab = ve.v

                def f_t2(te):
                    te.wait_ge(ve.sem, ve_ab)
                    for t in range(NT):
                        pe.inc(te.matmul(P1[:, t * TC:(t + 1) * TC], ones32[0:1, 0:128],
                                         arow[0:1, t * TC:(t + 1) * TC], start=True, stop=True))
                        pe.inc(te.matmul(P2[:, t * TC:(t + 1) * TC], ones32[0:1, 0:128],
                                         brow[0:1, t * TC:(t + 1) * TC], start=True, stop=True))
                block.tensor(f_t2)
                pe_bc = pe.v

                def f_v3(v):
                    v.wait_ge(pe.sem, pe_bc)
                    for g in range(G):
                        v.tensor_tensor(XN[:, g, :], XN[:, g, :], P1[:, :], ALU.mult)
                        i = v.tensor_tensor(XN[:, g, :], XN[:, g, :], P2[:, :], ALU.add)
                    ve.inc(i)
                block.vector(f_v3)

        def emit_attn_a():
            s0 = snap()
            with ExitStack() as actx:
                phase_id[0] += 1
                asb = lambda n, s, d: actx.enter_context(nc.sbuf_tensor(n + str(phase_id[0]), s, d))
                dmT = asb("dmT", [128, H, CH], BF16)
                dmv = asb("dmv", [128, H, CH], BF16)
                gqs = asb("gqs", [128, HP, CH], BF16)
                gcs = asb("gcs", [128, HP, CH], BF16)
                gks = asb("gks", [128, HP, CH], BF16)
                gcus = asb("gcus", [128, HP], F32)
                sel32 = asb("sel32", [H, G, 128], F32)
                pkv = asb("pkv", [128, 1024], BF16)
                wqb = asb("wqb", [128, 8, 128], BF16)
                wkb = asb("wkb", [128, 8, 128], BF16)
                wvb = asb("wvb", [128, 8, 128], BF16)
                qf = asb("qf", [128, T], BF16)
                kf = asb("kf", [128, T], BF16)
                kt = asb("kt", [128, T], BF16)
                vt = asb("vt", [128, T], BF16)
                ppt = asb("ppt", [128, T], BF16)
                qt1 = asb("qt1", [128, CH], BF16)
                ab1 = asb("ab1", [128, CH], BF16)
                ab2_ = asb("ab2_", [128, CH], BF16)
                tmc = asb("tmc", [128, CH], F32)
                carry = asb("carry", [128, 1], F32)
                N32 = asb("N32", [128, DH], F32)
                Nbf = asb("Nbf", [128, DH], BF16)

                def f_g(g_):
                    wsnap(g_, s0)
                    gw.inc(g_.dma_start(out=dmT[:], in_=dmaskT_d.rearrange("h s t -> s h t")), 16)
                    gw.inc(g_.dma_start(out=dmv[:], in_=dmask_d.rearrange("h s t -> s h t")), 16)
                    gw.inc(g_.dma_start(out=gqs[:], in_=gq_d.rearrange("h p c -> p h c")), 16)
                    gw.inc(g_.dma_start(out=gcs[:], in_=gc_d.rearrange("h p c -> p h c")), 16)
                    gw.inc(g_.dma_start(out=gks[:], in_=gk_d.rearrange("h p c -> p h c")), 16)
                    gw.inc(g_.dma_start(out=pkv[:], in_=pkv_d), 16)
                    gw.inc(g_.dma_start(out=wqb[:], in_=wq_d[:, 0:128].rearrange("(a p) c -> p a c", p=128)), 16)
                    gw.inc(g_.dma_start(out=wkb[:], in_=wk_d[:, 0:128].rearrange("(a p) c -> p a c", p=128)), 16)
                    gw.inc(g_.dma_start(out=wvb[:], in_=wv_d[:, 0:128].rearrange("(a p) c -> p a c", p=128)), 16)
                block.gpsimd(f_g)
                gw_consts = gw.v

                def f_sy(sy):
                    wsnap(sy, s0)
                    ld.inc(sy.dma_start(out=gcus[:], in_=gcu_d), 16)
                    ld.inc(sy.dma_start(out=sel32[:], in_=sel_d.rearrange("g h c -> h g c")), 16)
                block.sync(f_sy)
                ld_consts = ld.v

                gw_w = gw_consts
                for hp in range(HP):
                    hA, hB = 2 * hp, 2 * hp + 1
                    ve_prev = ve.v

                    def f_t1(te, hp=hp, gw_w=gw_w, ve_prev=ve_prev):
                        te.wait_ge(gw.sem, gw_w)
                        te.wait_ge(ve.sem, ve_prev)
                        if hp == 0:
                            wsnap(te, s0)
                            te.wait_ge(ld.sem, ld_consts)
                        for t in range(NT):
                            for k in range(G):
                                i = te.matmul(P1[:, t * TC:(t + 1) * TC], wqb[:, k, :],
                                              XN[:, k, t * TC:(t + 1) * TC], start=(k == 0), stop=(k == G - 1))
                            pe.inc(i)
                            for k in range(G):
                                i = te.matmul(P2[:, t * TC:(t + 1) * TC], wkb[:, k, :],
                                              XN[:, k, t * TC:(t + 1) * TC], start=(k == 0), stop=(k == G - 1))
                            pe.inc(i)
                    block.tensor(f_t1)
                    pe_qk = pe.v

                    def f_v1(v, pe_qk=pe_qk):
                        v.wait_ge(pe.sem, pe_qk)
                        ve.inc(v.tensor_copy(qf[:], P1[:, :]))
                        ve.inc(v.tensor_copy(kf[:], P2[:, :]))
                    block.vector(f_v1)
                    ve_qk = ve.v

                    def f_t2(te, ve_qk=ve_qk):
                        te.wait_ge(ve.sem, ve_qk)
                        for tt in range(NCH):
                            for k in range(G):
                                i = te.matmul(P1[:, tt * CH:(tt + 1) * CH], XN[:, k, tt * CH:(tt + 1) * CH],
                                              wkb[:, k, :], start=(k == 0), stop=(k == G - 1))
                            pe.inc(i)
                            for k in range(G):
                                i = te.matmul(P2[:, tt * CH:(tt + 1) * CH], XN[:, k, tt * CH:(tt + 1) * CH],
                                              wvb[:, k, :], start=(k == 0), stop=(k == G - 1))
                            pe.inc(i)
                    block.tensor(f_t2)
                    pe_kv = pe.v

                    def f_v2(v, pe_kv=pe_kv):
                        v.wait_ge(pe.sem, pe_kv)
                        ve.inc(v.tensor_copy(kt[:], P1[:, :]))
                        ve.inc(v.tensor_copy(vt[:], P2[:, :]))
                    block.vector(f_v2)
                    ve_kv = ve.v

                    if hp + 1 < HP:
                        def f_gp(g_, hp=hp, pe_kv=pe_kv):
                            g_.wait_ge(pe.sem, pe_kv)
                            c0 = 128 * (hp + 1)
                            gw.inc(g_.dma_start(out=wqb[:], in_=wq_d[:, c0:c0 + 128].rearrange("(a p) c -> p a c", p=128)), 16)
                            gw.inc(g_.dma_start(out=wkb[:], in_=wk_d[:, c0:c0 + 128].rearrange("(a p) c -> p a c", p=128)), 16)
                            gw.inc(g_.dma_start(out=wvb[:], in_=wv_d[:, c0:c0 + 128].rearrange("(a p) c -> p a c", p=128)), 16)
                        block.gpsimd(f_gp)
                        gw_w = gw.v

                    if DBG_STOP == "aa_proj":
                        continue

                    def f_t3(te, ve_kv=ve_kv, hA=hA, hB=hB):
                        te.wait_ge(ve.sem, ve_kv)
                        for I in range(NCH):
                            te.matmul(P1[:, I * 64:(I + 1) * 64], qf[0:64, I * CH:(I + 1) * CH],
                                      pkv[0:64, hA * 64:hA * 64 + 64], start=True, stop=True)
                            i = te.matmul(P1[:, 1024 + I * 64:1024 + (I + 1) * 64], qf[64:128, I * CH:(I + 1) * CH],
                                          pkv[64:128, hB * 64:hB * 64 + 64], start=True, stop=True)
                        pe.inc(i)
                    block.tensor(f_t3)
                    pe_pp = pe.v

                    def f_v3(v, pe_pp=pe_pp, hp=hp):
                        v.wait_ge(pe.sem, pe_pp)
                        pp4 = ppt.ap().rearrange("p (c h d) -> p c h d", h=2, d=64)
                        v.tensor_copy(pp4[:, :, 0, :], P1.ap().rearrange("p (x c d) -> p x c d", x=2, d=64)[:, 0, :, :])
                        ve.inc(v.tensor_copy(pp4[:, :, 1, :], P1.ap().rearrange("p (x c d) -> p x c d", x=2, d=64)[:, 1, :, :]))
                        ve.inc(v.memset(carry[:], 0.0))
                        if DBG_DUMP == "scrA" and DBG_STOP == "aa_pp" and hp == HP - 1:
                            v.wait_ge(ve.sem, ve.v)
                            ve.inc(v.tensor_copy(scrA[:], ppt[:]))
                        if DBG_DUMP == "pkv" and hp == HP - 1:
                            v.wait_ge(ve.sem, ve.v)
                            ve.inc(v.tensor_copy(scrA[:, 0:1024], pkv[:]))
                            ve.inc(v.tensor_copy(scrA[:, 1024:2048], qf[:, 0:1024]))
                    block.vector(f_v3)
                    ve_pp = ve.v

                    if DBG_STOP == "aa_pp":
                        continue

                    def f_t4(te, ve_pp=ve_pp, hA=hA, hB=hB):
                        te.wait_ge(ve.sem, ve_pp)
                        for I in range(NCH - 1, -1, -1):
                            te.matmul(P2[0:64, I * CH:(I + 1) * CH], ppt[:, I * CH:I * CH + 64],
                                      dmv[:, hA, :], start=True, stop=True)
                            pe.inc(te.matmul(P2[64:128, I * CH:(I + 1) * CH], ppt[:, I * CH + 64:(I + 1) * CH],
                                             dmv[:, hB, :], start=True, stop=True))
                    block.tensor(f_t4)
                    pe_scan0 = pe.v - NCH

                    if DBG_STOP == "aa_scan_t":
                        def f_v4x(v, pe_scan0=pe_scan0):
                            v.wait_ge(pe.sem, pe_scan0 + NCH)
                            ve.inc(v.tensor_copy(scrA[:], P2[:, :]))
                        block.vector(f_v4x)
                        continue

                    def f_v4(v, pe_scan0=pe_scan0, hp=hp):
                        v.wait_ge(pe.sem, pe_scan0 + NCH)
                        for idx, I in enumerate(range(NCH - 1, -1, -1)):
                            v.wait_ge(ve.sem, ve.v)
                            v.tensor_scalar(tmc[:], gcs[:, hp, :], carry[:, 0:1], None, ALU.mult)
                            ve.inc(v.tensor_tensor(scrA[:, I * CH:(I + 1) * CH], P2[:, I * CH:(I + 1) * CH], tmc[:], ALU.add))
                            v.wait_ge(ve.sem, ve.v)
                            ve.inc(v.tensor_copy(carry[:], scrA[:, I * CH:I * CH + 1]))
                    block.vector(f_v4)

                    pe_allscan = pe.v
                    if DBG_STOP == "aa_scan_nokhat":
                        continue

                    def f_v5(v, pe_allscan=pe_allscan, hp=hp):
                        v.wait_ge(pe.sem, pe_allscan)
                        for I in range(NCH):
                            i = v.tensor_tensor(ppt[:, I * CH:(I + 1) * CH], kt[:, I * CH:(I + 1) * CH],
                                                gks[:, hp, :], ALU.mult)
                        ve.inc(i)
                        v.memset(N32[:], 0.0)
                        ve.inc(v.memset(Nbf[:], 0.0))
                    block.vector(f_v5)
                    ve_khat = ve.v

                    if DBG_STOP == "aa_scan":
                        continue
                    pe_sc = {}
                    ve_ab_ = {}
                    ve_ret = {}
                    ve_nup = {}
                    for I in range(NCH):
                        def f_te(te, I=I, hp=hp):
                            if I == 0:
                                te.wait_ge(ve.sem, ve_khat)
                            else:
                                te.wait_ge(ve.sem, ve_ab_[I - 1])
                            te.matmul(P1[:, 0:128], kf[0:64, I * CH:(I + 1) * CH],
                                      qf[0:64, I * CH:(I + 1) * CH], start=True, stop=True)
                            pe.inc(te.matmul(P2[:, 0:128], kf[64:128, I * CH:(I + 1) * CH],
                                             qf[64:128, I * CH:(I + 1) * CH], start=True, stop=True))
                            pe_sc[I] = pe.v
                        block.tensor(f_te)

                        def f_ve(v, I=I, hp=hp):
                            v.wait_ge(pe.sem, pe_sc[I])
                            v.tensor_tensor(ab1[:], P1[:, 0:128], dmT[:, 2 * hp, :], ALU.mult)
                            v.tensor_tensor(ab2_[:], P2[:, 0:128], dmT[:, 2 * hp + 1, :], ALU.mult)
                            ve.inc(v.tensor_tensor(qt1[:], qf[:, I * CH:(I + 1) * CH], gqs[:, hp, :], ALU.mult))
                            ve_ab_[I] = ve.v
                        block.vector(f_ve)

                        def f_te2(te, I=I):
                            te.wait_ge(ve.sem, ve_ab_[I])
                            if I >= 1:
                                te.wait_ge(ve.sem, ve_ret[I - 1])
                            te.matmul(P1[0:64, 512:640], vt[:, I * CH:I * CH + 64], ab1[:], start=True, stop=False)
                            te.matmul(P1[0:64, 512:640], Nbf[0:64, :], qt1[0:64, :], start=False, stop=True)
                            te.matmul(P1[64:128, 512:640], vt[:, I * CH + 64:(I + 1) * CH], ab2_[:], start=True, stop=False)
                            pe.inc(te.matmul(P1[64:128, 512:640], Nbf[64:128, :], qt1[64:128, :], start=False, stop=True))
                            pe_sc[(I, 'r')] = pe.v
                            nb = 1024 if I % 2 == 0 else 1536
                            if I >= 2:
                                te.wait_ge(ve.sem, ve_nup[I - 2])
                            te.matmul(P1[0:64, nb:nb + 64], ppt[:, I * CH:I * CH + 64],
                                      vt[:, I * CH:I * CH + 64], start=True, stop=True)
                            pe.inc(te.matmul(P1[64:128, nb:nb + 64], ppt[:, I * CH + 64:(I + 1) * CH],
                                             vt[:, I * CH + 64:(I + 1) * CH], start=True, stop=True))
                            pe_sc[(I, 'n')] = pe.v
                        block.tensor(f_te2)

                        def f_ve2(v, I=I, hp=hp):
                            v.wait_ge(pe.sem, pe_sc[(I, 'r')])
                            ve.inc(v.tensor_tensor(FM[:, hp, I * CH:(I + 1) * CH], P1[:, 512:640],
                                                   scrA[:, I * CH:(I + 1) * CH], ALU.add))
                            ve_ret[I] = ve.v
                            nb = 1024 if I % 2 == 0 else 1536
                            v.wait_ge(pe.sem, pe_sc[(I, 'n')])
                            v.scalar_tensor_tensor(N32[:], N32[:], gcus[:, hp:hp + 1], P1[:, nb:nb + 64], ALU.mult, ALU.add)
                            v.wait_ge(ve.sem, ve.v)
                            ve.inc(v.tensor_copy(Nbf[:], N32[:]))
                            ve_nup[I] = ve.v
                        block.vector(f_ve2)

                if DBG_STOP == "aa_fwd":
                    return
                sA = snap()

                def f_s3(se_):
                    wsnap(se_, sA)
                    for g in range(G):
                        i = se_.activation(XN[:, g, :], FM[:, g, :], AF.Square)
                    se.inc(i)
                block.scalar(f_s3)
                se_sq = se.v

                def f_t5(te):
                    te.wait_ge(se.sem, se_sq)
                    first = True
                    for g in range(G):
                        for t in range(NT):
                            i = te.matmul(P1[0:1, 0:TC], onescol[:], FM[:, g, t * TC:(t + 1) * TC],
                                          start=first, stop=(g == G - 1 and t == NT - 1))
                            first = False
                    pe.inc(i)
                    first = True
                    for g in range(G):
                        for t in range(NT):
                            i = te.matmul(P2[0:1, 0:TC], onescol[:], XN[:, g, t * TC:(t + 1) * TC],
                                          start=first, stop=(g == G - 1 and t == NT - 1))
                            first = False
                    pe.inc(i)
                block.tensor(f_t5)
                pe_gn = pe.v

                def f_v6(v):
                    v.wait_ge(pe.sem, pe_gn)
                    ve.inc(v.reduce_sum(vals[0:1, 0:1], P1[0:1, 0:TC], axis=AX))
                    ve.inc(v.reduce_sum(vals[0:1, 1:2], P2[0:1, 0:TC], axis=AX))
                    v.wait_ge(ve.sem, ve.v)
                    v.tensor_scalar(vals[0:1, 2:3], vals[0:1, 0:1], 1.0 / NTOK_GN, None, ALU.mult)
                    ve.inc(v.tensor_scalar(vals[0:1, 3:4], vals[0:1, 1:2], 1.0 / NTOK_GN, None, ALU.mult))
                    v.wait_ge(ve.sem, ve.v)
                    ve.inc(v.tensor_tensor(vals[0:1, 4:5], vals[0:1, 2:3], vals[0:1, 2:3], ALU.mult))
                    v.wait_ge(ve.sem, ve.v)
                    ve.inc(v.tensor_tensor(vals[0:1, 5:6], vals[0:1, 3:4], vals[0:1, 4:5], ALU.subtract))
                block.vector(f_v6)
                ve_gnv = ve.v

                def f_s4(se_):
                    se_.wait_ge(ve.sem, ve_gnv)
                    se.inc(se_.activation(vals[0:1, 6:7], vals[0:1, 5:6], AF.Sqrt, bias=epsc[0:1, 0:1]))
                block.scalar(f_s4)
                se_gns = se.v

                def f_v7(v):
                    v.wait_ge(se.sem, se_gns)
                    ve.inc(v.reciprocal(vals[0:1, 8:9], vals[0:1, 6:7]))
                    v.wait_ge(ve.sem, ve.v)
                    ve.inc(v.tensor_copy(vals[0:1, 9:10], vals[0:1, 2:3]))
                block.vector(f_v7)
                ve_gnr = ve.v

                def f_t6(te):
                    te.wait_ge(ve.sem, ve_gnr)
                    pe.inc(te.matmul(P1[0:16, 0:2], ones32[0:1, 0:16], vals[0:1, 8:10], start=True, stop=True))
                block.tensor(f_t6)
                pe_b16 = pe.v

                def f_v8(v):
                    v.wait_ge(pe.sem, pe_b16)
                    ve.inc(v.tensor_copy(rb16[:], P1[0:16, 0:2]))
                    v.wait_ge(ve.sem, ve.v)
                    ve.inc(v.tensor_tensor(stat2[:, 0:1], gnw16[:], rb16[:, 0:1], ALU.mult))
                    v.wait_ge(ve.sem, ve.v)
                    ve.inc(v.tensor_tensor(stat2[:, 1:2], stat2[:, 0:1], rb16[:, 1:2], ALU.mult))
                    v.wait_ge(ve.sem, ve.v)
                    ve.inc(v.tensor_tensor(stat2[:, 1:2], gnb16[:], stat2[:, 1:2], ALU.subtract))
                block.vector(f_v8)
                ve_st2 = ve.v

                def f_t7(te):
                    te.wait_ge(ve.sem, ve_st2)
                    for g in range(G):
                        i = te.matmul(P2[:, 2 * g:2 * g + 2], sel32[:, g, :], stat2[:], start=True, stop=True)
                    pe.inc(i)
                block.tensor(f_t7)
                pe_sel = pe.v

                def f_v9(v):
                    v.wait_ge(pe.sem, pe_sel)
                    ve.inc(v.tensor_copy(stcol[:], P2[:, 0:2 * G]))
                    v.wait_ge(ve.sem, ve.v)
                    for g in range(G):
                        v.scalar_tensor_tensor(XB[:, g, :], FM[:, g, :], stcol[:, 2 * g:2 * g + 1],
                                               XB[:, g, :], ALU.mult, ALU.add)
                        i = v.tensor_scalar(XB[:, g, :], XB[:, g, :], stcol[:, 2 * g + 1:2 * g + 2], None, ALU.add)
                    ve.inc(i)
                block.vector(f_v9)

        def emit_attn_b():
            s0 = snap()
            with ExitStack() as actx:
                phase_id[0] += 1
                asb = lambda n, s, d: actx.enter_context(nc.sbuf_tensor(n + str(phase_id[0]), s, d))
                wqb = asb("bwq", [128, 8, 128], BF16)
                wkb = asb("bwk", [128, 8, 128], BF16)
                wvb = asb("bwv", [128, 8, 128], BF16)
                owT = asb("owT", [128, 8, 1024], BF16)
                wbq = asb("wbq", [1, 3072], BF16)
                obq = asb("obq", [1, 1024], BF16)
                qf = asb("bqf", [128, T], BF16)
                kf = asb("bkf", [128, T], BF16)
                vt = asb("bvt", [128, NCH, 130], BF16)
                eb = asb("eb", [128, T], BF16)
                zrow = asb("zrow", [1, TC], F32)
                zb = asb("zb", [64, TC], F32)

                def f_g(g_):
                    wsnap(g_, s0)
                    gw.inc(g_.dma_start(out=wbq[:], in_=wb_d), 16)
                    gw.inc(g_.dma_start(out=obq[:], in_=ob_d), 16)
                    gw.inc(g_.dma_start(out=wqb[:], in_=wwT_d[:, 0:128].rearrange("(a p) c -> p a c", p=128)), 16)
                    gw.inc(g_.dma_start(out=wkb[:], in_=wwT_d[:, 1024:1152].rearrange("(a p) c -> p a c", p=128)), 16)
                    gw.inc(g_.dma_start(out=wvb[:], in_=wwT_d[:, 2048:2176].rearrange("(a p) c -> p a c", p=128)), 16)
                    gw.inc(g_.dma_start(out=owT[:], in_=owT_d.rearrange("(a p) c -> p a c", p=128)), 16)
                    gp.inc(g_.memset(vt.ap().rearrange("p t (h c) -> p t h c", c=65)[:, :, :, 64:65], 1.0), 1)
                block.gpsimd(f_g)
                gw_w = gw.v
                gp_ones = gp.v

                for hp in range(HP):
                    ve_prev = ve.v

                    def f_t1(te, hp=hp, gw_w=gw_w, ve_prev=ve_prev):
                        te.wait_ge(gw.sem, gw_w)
                        te.wait_ge(ve.sem, ve_prev)
                        if hp == 0:
                            wsnap(te, s0)
                        for t in range(NT):
                            for k in range(G):
                                te.matmul(P1[:, t * TC:(t + 1) * TC], wqb[:, k, :],
                                          XN[:, k, t * TC:(t + 1) * TC], start=(k == 0), stop=False)
                            pe.inc(te.matmul(P1[:, t * TC:(t + 1) * TC], wbq[0:1, 128 * hp:128 * hp + 128],
                                             onesbf[0:1, 0:TC], start=False, stop=True))
                            for k in range(G):
                                te.matmul(P2[:, t * TC:(t + 1) * TC], wkb[:, k, :],
                                          XN[:, k, t * TC:(t + 1) * TC], start=(k == 0), stop=False)
                            pe.inc(te.matmul(P2[:, t * TC:(t + 1) * TC], wbq[0:1, 1024 + 128 * hp:1024 + 128 * hp + 128],
                                             onesbf[0:1, 0:TC], start=False, stop=True))
                    block.tensor(f_t1)
                    pe_qk = pe.v

                    def f_v1(v, pe_qk=pe_qk):
                        v.wait_ge(pe.sem, pe_qk)
                        ve.inc(v.tensor_copy(qf[:], P1[:, :]))
                        ve.inc(v.tensor_copy(kf[:], P2[:, :]))
                    block.vector(f_v1)
                    ve_qk = ve.v

                    def f_t2(te, hp=hp, ve_qk=ve_qk):
                        te.wait_ge(ve.sem, ve_qk)
                        if hp == 0:
                            te.wait_ge(gp.sem, gp_ones)
                        for tt in range(NCH):
                            for k in range(G):
                                te.matmul(P1[:, tt * CH:(tt + 1) * CH], XN[:, k, tt * CH:(tt + 1) * CH],
                                          wvb[:, k, :], start=(k == 0), stop=False)
                            i = te.matmul(P1[:, tt * CH:(tt + 1) * CH], onesbf[0:1, 0:128],
                                          wbq[0:1, 2048 + 128 * hp:2048 + 128 * hp + 128], start=False, stop=True)
                            pe.inc(i)
                    block.tensor(f_t2)
                    pe_v = pe.v

                    def f_v2(v, pe_v=pe_v):
                        v.wait_ge(pe.sem, pe_v)
                        p3 = P1.ap().rearrange("p (t c) -> p t c", c=CH)
                        v.tensor_copy(vt[:, :, 0:64], p3[:, :, 0:64])
                        ve.inc(v.tensor_copy(vt[:, :, 65:129], p3[:, :, 64:128]))
                    block.vector(f_v2)
                    ve_v = ve.v

                    if hp + 1 < HP:
                        def f_gp(g_, hp=hp, pe_v=pe_v):
                            g_.wait_ge(pe.sem, pe_v)
                            c0 = 128 * (hp + 1)
                            gw.inc(g_.dma_start(out=wqb[:], in_=wwT_d[:, c0:c0 + 128].rearrange("(a p) c -> p a c", p=128)), 16)
                            gw.inc(g_.dma_start(out=wkb[:], in_=wwT_d[:, 1024 + c0:1024 + c0 + 128].rearrange("(a p) c -> p a c", p=128)), 16)
                            gw.inc(g_.dma_start(out=wvb[:], in_=wwT_d[:, 2048 + c0:2048 + c0 + 128].rearrange("(a p) c -> p a c", p=128)), 16)
                        block.gpsimd(f_gp)
                        gw_w = gw.v

                    for hloc in range(2):
                        hbase = 64 * hloc
                        voff = 65 * hloc
                        for t in range(NT):
                            ob = 0 if (t % 2 == 0) else 512
                            se_exp = {}
                            pe_av = {}
                            last_norm = ve.v
                            for sb_ in range(4):
                                def f_ts(te, sb_=sb_, hbase=hbase, t=t, ve_v=ve_v, last_norm=last_norm):
                                    if sb_ == 0:
                                        te.wait_ge(ve.sem, max(ve_v, last_norm))
                                    if sb_ >= 1:
                                        te.wait_ge(se.sem, se_exp[sb_ - 1])
                                    for si in range(4):
                                        s = sb_ * 4 + si
                                        i = te.matmul(P1[:, si * TC:(si + 1) * TC],
                                                      kf[hbase:hbase + 64, s * CH:(s + 1) * CH],
                                                      qf[hbase:hbase + 64, t * TC:(t + 1) * TC], start=True, stop=True)
                                    pe.inc(i)
                                    pe_av[('s', sb_)] = pe.v
                                block.tensor(f_ts)

                                def f_se(se_, sb_=sb_):
                                    se_.wait_ge(pe.sem, pe_av[('s', sb_)])
                                    if sb_ >= 1:
                                        se_.wait_ge(pe.sem, pe_av[('a', sb_ - 1)])
                                    se.inc(se_.activation(eb[:], P1[:, :], AF.Exp, scale=0.125))
                                    se_exp[sb_] = se.v
                                block.scalar(f_se)

                                def f_ta(te, sb_=sb_, voff=voff, ob=ob):
                                    te.wait_ge(se.sem, se_exp[sb_])
                                    for si in range(4):
                                        s = sb_ * 4 + si
                                        i = te.matmul(P2[0:65, ob:ob + TC], vt[:, s, voff:voff + 65],
                                                      eb[:, si * TC:(si + 1) * TC],
                                                      start=(sb_ == 0 and si == 0), stop=(sb_ == 3 and si == 3))
                                    pe.inc(i)
                                    pe_av[('a', sb_)] = pe.v
                                block.tensor(f_ta)

                            def f_vn(v, ob=ob):
                                v.wait_ge(pe.sem, pe_av[('a', 3)])
                                ve.inc(v.reciprocal(zrow[:], P2[64:65, ob:ob + TC]))
                            block.vector(f_vn)
                            ve_r = ve.v

                            def f_tz(te, ve_r=ve_r):
                                te.wait_ge(ve.sem, ve_r)
                                pe.inc(te.matmul(P2[0:64, 1024:1024 + TC], ones32[0:1, 0:64],
                                                 zrow[0:1, 0:TC], start=True, stop=True))
                            block.tensor(f_tz)
                            pe_z = pe.v

                            def f_vz(v, hp=hp, hbase=hbase, t=t, ob=ob, pe_z=pe_z):
                                v.wait_ge(pe.sem, pe_z)
                                ve.inc(v.tensor_copy(zb[:], P2[0:64, 1024:1024 + TC]))
                                v.wait_ge(ve.sem, ve.v)
                                ve.inc(v.tensor_tensor(FM[hbase:hbase + 64, hp, t * TC:(t + 1) * TC],
                                                       P2[0:64, ob:ob + TC], zb[:], ALU.mult))
                            block.vector(f_vz)

                ve_out = {}
                for m in range(G):
                    ve_prev2 = ve.v

                    def f_t8(te, m=m, ve_prev2=ve_prev2):
                        if m < 2:
                            te.wait_ge(ve.sem, ve_prev2)
                        else:
                            te.wait_ge(ve.sem, ve_out[m - 2])
                        PP = P1 if m % 2 == 0 else P2
                        for t in range(NT):
                            for k in range(G):
                                te.matmul(PP[:, t * TC:(t + 1) * TC], owT[:, k, 128 * m:128 * m + 128],
                                          FM[:, k, t * TC:(t + 1) * TC], start=(k == 0), stop=False)
                            i = te.matmul(PP[:, t * TC:(t + 1) * TC], obq[0:1, 128 * m:128 * m + 128],
                                          onesbf[0:1, 0:TC], start=False, stop=True)
                        pe.inc(i)
                    block.tensor(f_t8)
                    pe_out = pe.v

                    def f_v10(v, m=m, pe_out=pe_out):
                        v.wait_ge(pe.sem, pe_out)
                        PP = P1 if m % 2 == 0 else P2
                        ve.inc(v.tensor_tensor(XB[:, m, :], XB[:, m, :], PP[:, :], ALU.add))
                        ve_out[m] = ve.v
                    block.vector(f_v10)

        def emit_ffn():
            s0 = snap()
            with ExitStack() as fctx:
                phase_id[0] += 1
                fsb = lambda n, s, d: fctx.enter_context(nc.sbuf_tensor(n + str(phase_id[0]), s, d))
                wub = fsb("wub", [128, 8, SLAB], BF16)
                wgb = fsb("wgb", [128, 8, SLAB], BF16)
                wdb = fsb("wdb", [128, 4, 1024], BF16)
                LG = fsb("LG", [128, 128], F32)
                SBB = fsb("SBB", [128, 128], F32)
                et = fsb("et", [128, 128], F32)
                mk = fsb("mk", [128, 128], F32)
                pa = fsb("pa", [128, 128], F32)
                pb = fsb("pb", [128, 128], F32)
                ctm = fsb("ctm", [128, 128], F32)
                lgt = fsb("lgt", [128, 128], F32)
                abt_a = fsb("abt_a", [128, 16], F32)
                abt_b = fsb("abt_b", [128, 16], F32)
                mx16 = fsb("mx16", [128, 16], F32)
                z16 = fsb("z16", [128, 16], F32)
                c16 = fsb("c16", [128, 16], F32)
                cbrow = fsb("cbrow", [1, T], F32)
                scrB = fsb("scrB", [128, T], F32)

                def f_t1(te):
                    wsnap(te, s0)
                    for t in range(NT):
                        for k in range(G):
                            i = te.matmul(P1[0:8, t * TC:(t + 1) * TC], wr32[:, k, :],
                                          XB[:, k, t * TC:(t + 1) * TC], start=(k == 0), stop=(k == G - 1))
                        pe.inc(i)
                    for k in range(G):
                        i = te.matmul(P2[0:1, 0:8], onescol32[:], wr32[:, k, :], start=(k == 0), stop=(k == G - 1))
                    pe.inc(i)
                block.tensor(f_t1)
                pe_raw = pe.v

                def f_v1(v):
                    v.wait_ge(pe.sem, pe_raw)
                    ve.inc(v.tensor_copy(scrA[0:8, :], P1[0:8, :]))
                    ve.inc(v.tensor_copy(wrsrow[:], P2[0:1, 0:8]))
                block.vector(f_v1)
                ve_raw = ve.v

                def f_t2(te):
                    te.wait_ge(ve.sem, ve_raw)
                    for tt in range(NCH):
                        pe.inc(te.transpose(P2[:, tt * 8:(tt + 1) * 8], scrA[0:8, tt * CH:(tt + 1) * CH], id32[0:8, 0:8]))
                    for tt in range(NCH):
                        te.transpose(P2[:, 256 + tt:256 + tt + 1], arow[0:1, tt * CH:(tt + 1) * CH], id32[0:1, 0:1])
                        pe.inc(te.transpose(P2[:, 288 + tt:288 + tt + 1], brow[0:1, tt * CH:(tt + 1) * CH], id32[0:1, 0:1]))
                    pe.inc(te.matmul(P1[:, 1024:1032], ones32[0:1, 0:128], wrsrow[0:1, 0:8], start=True, stop=True))
                block.tensor(f_t2)
                pe_tp = pe.v

                def f_v2(v):
                    v.wait_ge(pe.sem, pe_tp)
                    v.tensor_copy(lgt[:], P2[:, 0:128])
                    v.tensor_copy(abt_a[:], P2[:, 256:272])
                    v.tensor_copy(abt_b[:], P2[:, 288:304])
                    ve.inc(v.tensor_copy(wrsbc[:], P1[:, 1024:1032]))
                    v.wait_ge(ve.sem, ve.v)
                    lg3 = LG.ap().rearrange("p (t e) -> p t e", e=8)
                    raw3 = lgt.ap().rearrange("p (t e) -> p t e", e=8)
                    a3 = abt_a.ap().rearrange("p (t o) -> p t o", o=1)
                    b3 = abt_b.ap().rearrange("p (t o) -> p t o", o=1)
                    for jj in range(8):
                        v.tensor_tensor(lg3[:, :, jj:jj + 1], raw3[:, :, jj:jj + 1], a3, ALU.mult)
                    ve.inc(v.engine_nop())
                    v.wait_ge(ve.sem, ve.v)
                    for jj in range(8):
                        v.scalar_tensor_tensor(lg3[:, :, jj:jj + 1], b3, wrsbc[:, jj:jj + 1],
                                               lg3[:, :, jj:jj + 1], ALU.mult, ALU.add)
                    ve.inc(v.engine_nop())
                    v.wait_ge(ve.sem, ve.v)
                    mx3 = mx16.ap().rearrange("p (t o) -> p t o", o=1)
                    ve.inc(v.reduce_max(mx3, lg3, axis=AX))
                    v.wait_ge(ve.sem, ve.v)
                    sb3 = SBB.ap().rearrange("p (t e) -> p t e", e=8)
                    for jj in range(8):
                        v.tensor_tensor(sb3[:, :, jj:jj + 1], lg3[:, :, jj:jj + 1], mx3, ALU.subtract)
                    ve.inc(v.engine_nop())
                block.vector(f_v2)
                ve_sub = ve.v

                def f_s1(se_):
                    se_.wait_ge(ve.sem, ve_sub)
                    se.inc(se_.activation(et[:], SBB[:], AF.Exp))
                block.scalar(f_s1)
                se_expv = se.v

                def f_v3(v):
                    v.wait_ge(se.sem, se_expv)
                    et3 = et.ap().rearrange("p (t e) -> p t e", e=8)
                    z3 = z16.ap().rearrange("p (t o) -> p t o", o=1)
                    ve.inc(v.reduce_sum(z3, et3, axis=AX))
                    v.wait_ge(ve.sem, ve.v)
                    ve.inc(v.reciprocal(z16[:], z16[:]))
                    v.wait_ge(ve.sem, ve.v)
                    ve.inc(v.tensor_scalar(c16[:], z16[:], 2.0, None, ALU.mult))
                    ve.inc(v.tensor_scalar(mk[:], et[:], 1.0, None, ALU.is_equal))
                    v.wait_ge(ve.sem, ve.v)
                    mk3 = mk.ap().rearrange("p (t e) -> p t e", e=8)
                    pa3 = pa.ap().rearrange("p (t e) -> p t e", e=8)
                    pb3 = pb.ap().rearrange("p (t e) -> p t e", e=8)
                    v.tensor_copy(pa3[:, :, 0:1], mk3[:, :, 0:1])
                    ve.inc(v.tensor_tensor(pa3[:, :, 1:8], mk3[:, :, 1:8], mk3[:, :, 0:7], ALU.add))
                    v.wait_ge(ve.sem, ve.v)
                    v.tensor_copy(pb3[:, :, 0:2], pa3[:, :, 0:2])
                    ve.inc(v.tensor_tensor(pb3[:, :, 2:8], pa3[:, :, 2:8], pa3[:, :, 0:6], ALU.add))
                    v.wait_ge(ve.sem, ve.v)
                    v.tensor_copy(pa3[:, :, 0:4], pb3[:, :, 0:4])
                    ve.inc(v.tensor_tensor(pa3[:, :, 4:8], pb3[:, :, 4:8], pb3[:, :, 0:4], ALU.add))
                    v.wait_ge(ve.sem, ve.v)
                    ve.inc(v.tensor_scalar(pb[:], pa[:], 1.0, None, ALU.is_equal))
                    v.wait_ge(ve.sem, ve.v)
                    ve.inc(v.tensor_tensor(mk[:], mk[:], pb[:], ALU.mult))
                    v.wait_ge(ve.sem, ve.v)
                    ctm3 = ctm.ap().rearrange("p (t e) -> p t e", e=8)
                    c3 = c16.ap().rearrange("p (t o) -> p t o", o=1)
                    for jj in range(8):
                        v.tensor_tensor(ctm3[:, :, jj:jj + 1], mk3[:, :, jj:jj + 1], c3, ALU.mult)
                    ve.inc(v.engine_nop())
                block.vector(f_v3)
                ve_ctm = ve.v

                def f_t3(te):
                    te.wait_ge(ve.sem, ve_ctm)
                    for tt in range(NCH):
                        pe.inc(te.transpose(P1[0:8, tt * CH:(tt + 1) * CH], ctm[:, tt * 8:(tt + 1) * 8], id32[:]))
                block.tensor(f_t3)
                pe_ct = pe.v

                def f_v4(v):
                    v.wait_ge(pe.sem, pe_ct)
                    ve.inc(v.tensor_copy(scrA[0:8, :], P1[0:8, :]))
                block.vector(f_v4)
                ve_crows = ve.v

                for e in range(8):
                    shared = (e == 7)
                    if not shared:
                        pe_now = pe.v

                        def f_sy(sy, e=e, pe_now=pe_now):
                            sy.wait_ge(ve.sem, ve_crows)
                            sy.wait_ge(pe.sem, pe_now)
                            ld.inc(sy.dma_start(out=cbrow[:], in_=scrA[e:e + 1, :]), 16)
                        block.sync(f_sy)
                        ld_cb = ld.v
                        ve_pp2 = ve.v

                        def f_tc(te, ld_cb=ld_cb, ve_pp2=ve_pp2):
                            te.wait_ge(ld.sem, ld_cb)
                            te.wait_ge(ve.sem, ve_pp2)
                            for t in range(NT):
                                i = te.matmul(P1[:, t * TC:(t + 1) * TC], ones32[0:1, 0:128],
                                              cbrow[0:1, t * TC:(t + 1) * TC], start=True, stop=True)
                            pe.inc(i)
                        block.tensor(f_tc)
                        pe_cb = pe.v

                        def f_vc(v, pe_cb=pe_cb):
                            v.wait_ge(pe.sem, pe_cb)
                            ve.inc(v.tensor_copy(scrB[:], P1[:, :]))
                        block.vector(f_vc)
                    ve_cbc = ve.v

                    def f_gw0(g_, e=e, shared=shared):
                        g_.wait_ge(pe.sem, pe.v)
                        if shared:
                            gw.inc(g_.dma_start(out=wub[:], in_=wus_d[:, 0:SLAB].rearrange("(k p) c -> p k c", p=128)), 16)
                            gw.inc(g_.dma_start(out=wgb[:], in_=wgs_d[:, 0:SLAB].rearrange("(k p) c -> p k c", p=128)), 16)
                            gw.inc(g_.dma_start(out=wdb[:], in_=wds_d[0:SLAB, :].rearrange("(k p) c -> p k c", p=128)), 16)
                        else:
                            gw.inc(g_.dma_start(out=wub[:], in_=wu_d[e, :, 0:SLAB].rearrange("(k p) c -> p k c", p=128)), 16)
                            gw.inc(g_.dma_start(out=wgb[:], in_=wg_d[e, :, 0:SLAB].rearrange("(k p) c -> p k c", p=128)), 16)
                            gw.inc(g_.dma_start(out=wdb[:], in_=wd_d[e, 0:SLAB, :].rearrange("(k p) c -> p k c", p=128)), 16)
                    block.gpsimd(f_gw0)

                    for s in range(NSLAB):
                        gw_u, gw_g, gw_d = gw.v - 32, gw.v - 16, gw.v

                        ve_up = {}
                        for m in range(4):
                            def f_tu(te, m=m, s=s, gw_u=gw_u, ve_cbc=ve_cbc):
                                if m == 0:
                                    te.wait_ge(gw.sem, gw_u)
                                    if s == 0:
                                        te.wait_ge(ve.sem, ve_cbc)
                                if m >= 2:
                                    te.wait_ge(ve.sem, ve_up[m - 2])
                                PP = P1 if m % 2 == 0 else P2
                                for t in range(NT):
                                    for k in range(G):
                                        i = te.matmul(PP[:, t * TC:(t + 1) * TC], wub[:, k, m * 128:(m + 1) * 128],
                                                      XN[:, k, t * TC:(t + 1) * TC], start=(k == 0), stop=(k == G - 1))
                                    pe.inc(i)
                            block.tensor(f_tu)
                            pe_um = pe.v

                            def f_vu(v, m=m, pe_um=pe_um, shared=shared):
                                v.wait_ge(pe.sem, pe_um)
                                PP = P1 if m % 2 == 0 else P2
                                if shared:
                                    ve.inc(v.tensor_copy(FM[:, m, :], PP[:, :]))
                                else:
                                    ve.inc(v.tensor_tensor(FM[:, m, :], PP[:, :], scrB[:], ALU.mult))
                                ve_up[m] = ve.v
                            block.vector(f_vu)

                        se_gt = {}
                        for m in range(4):
                            def f_tg(te, m=m, gw_g=gw_g):
                                if m == 0:
                                    te.wait_ge(gw.sem, gw_g)
                                    te.wait_ge(ve.sem, ve_up[3])
                                if m >= 2:
                                    te.wait_ge(se.sem, se_gt[m - 2])
                                PP = P1 if m % 2 == 0 else P2
                                for t in range(NT):
                                    for k in range(G):
                                        i = te.matmul(PP[:, t * TC:(t + 1) * TC], wgb[:, k, m * 128:(m + 1) * 128],
                                                      XN[:, k, t * TC:(t + 1) * TC], start=(k == 0), stop=(k == G - 1))
                                    pe.inc(i)
                            block.tensor(f_tg)
                            pe_gm = pe.v

                            def f_sg(se_, m=m, pe_gm=pe_gm):
                                se_.wait_ge(pe.sem, pe_gm)
                                se.inc(se_.activation(FM[:, 4 + m, :], (P1 if m % 2 == 0 else P2)[:, :], AF.Silu))
                                se_gt[m] = se.v
                            block.scalar(f_sg)

                        def f_vh(v, se_all=None):
                            v.wait_ge(se.sem, se_gt[3])
                            for m in range(4):
                                i = v.tensor_tensor(FM[:, m, :], FM[:, m, :], FM[:, 4 + m, :], ALU.mult)
                            ve.inc(i)
                        block.vector(f_vh)
                        ve_hid = ve.v

                        if DBG_DUMP == "hid0" and e == 0 and s == 0:
                            def f_vdbg2(v):
                                v.wait_ge(ve.sem, ve_hid)
                                ve.inc(v.tensor_copy(scrA[:], FM[:, 0, :]))
                            block.vector(f_vdbg2)

                        pe_gt_done = pe.v

                        def f_gwn(g_, e=e, s=s, shared=shared, pe_gt_done=pe_gt_done):
                            if s + 1 < NSLAB:
                                g_.wait_ge(pe.sem, pe_gt_done)
                                c0 = (s + 1) * SLAB
                                if shared:
                                    gw.inc(g_.dma_start(out=wub[:], in_=wus_d[:, c0:c0 + SLAB].rearrange("(k p) c -> p k c", p=128)), 16)
                                    gw.inc(g_.dma_start(out=wgb[:], in_=wgs_d[:, c0:c0 + SLAB].rearrange("(k p) c -> p k c", p=128)), 16)
                                else:
                                    gw.inc(g_.dma_start(out=wub[:], in_=wu_d[e, :, c0:c0 + SLAB].rearrange("(k p) c -> p k c", p=128)), 16)
                                    gw.inc(g_.dma_start(out=wgb[:], in_=wg_d[e, :, c0:c0 + SLAB].rearrange("(k p) c -> p k c", p=128)), 16)
                        block.gpsimd(f_gwn)

                        ve_dn = {}
                        for mo in range(G):
                            def f_td(te, mo=mo, gw_d=gw_d, ve_hid=ve_hid):
                                if mo == 0:
                                    te.wait_ge(gw.sem, gw_d)
                                    te.wait_ge(ve.sem, ve_hid)
                                if mo >= 2:
                                    te.wait_ge(ve.sem, ve_dn[mo - 2])
                                PP = P1 if mo % 2 == 0 else P2
                                for t in range(NT):
                                    for k in range(4):
                                        i = te.matmul(PP[:, t * TC:(t + 1) * TC], wdb[:, k, mo * 128:(mo + 1) * 128],
                                                      FM[:, k, t * TC:(t + 1) * TC], start=(k == 0), stop=(k == 3))
                                    pe.inc(i)
                            block.tensor(f_td)
                            pe_dm = pe.v

                            def f_vd(v, mo=mo, pe_dm=pe_dm):
                                v.wait_ge(pe.sem, pe_dm)
                                PP = P1 if mo % 2 == 0 else P2
                                ve.inc(v.tensor_tensor(XB[:, mo, :], XB[:, mo, :], PP[:, :], ALU.add))
                                ve_dn[mo] = ve.v
                            block.vector(f_vd)

                        pe_dn_done = pe.v

                        if DBG_DUMP == "cbc" and e == 6 and s == 0:
                            def f_vdbg(v):
                                v.wait_ge(ve.sem, ve.v)
                                ve.inc(v.tensor_copy(scrA[:], scrB[:]))
                            block.vector(f_vdbg)

                        def f_gwd(g_, e=e, s=s, shared=shared, pe_dn_done=pe_dn_done):
                            if s + 1 < NSLAB:
                                g_.wait_ge(pe.sem, pe_dn_done)
                                c0 = (s + 1) * SLAB
                                if shared:
                                    gw.inc(g_.dma_start(out=wdb[:], in_=wds_d[c0:c0 + SLAB, :].rearrange("(k p) c -> p k c", p=128)), 16)
                                else:
                                    gw.inc(g_.dma_start(out=wdb[:], in_=wd_d[e, c0:c0 + SLAB, :].rearrange("(k p) c -> p k c", p=128)), 16)
                        block.gpsimd(f_gwd)

        def emit_store():
            s0 = snap()
            if DBG_DUMP == "abrows":
                def f_sab(sy):
                    wsnap(sy, s0)
                    st.inc(sy.dma_start(out=y_d[0:1, :], in_=arow[:]), 16)
                    st.inc(sy.dma_start(out=y_d[1:2, :], in_=brow[:]), 16)
                    sy.wait_ge(st.sem, st.v)
                block.sync(f_sab)
                return
            if DBG_DUMP in ("scrA", "pkv", "cbc", "hid0"):
                def f_sd(sy):
                    wsnap(sy, s0)
                    st.inc(sy.dma_start(out=y_d[0:128, :], in_=scrA[:]), 16)
                    sy.wait_ge(st.sem, st.v)
                block.sync(f_sd)
                return
            if DBG_DUMP is not None:
                buf = {"FM": FM, "XN": XN}[DBG_DUMP]
                for g in range(G):
                    stv = st.v

                    def f_vd(v, g=g, stv=stv):
                        if g == 0:
                            wsnap(v, s0)
                        v.wait_ge(st.sem, stv)
                        ve.inc(v.tensor_copy(scrA[:], buf[:, g, :]))
                    block.vector(f_vd)
                    vev = ve.v

                    def f_sd(sy, g=g, vev=vev):
                        sy.wait_ge(ve.sem, vev)
                        st.inc(sy.dma_start(out=y_d[g * 128:(g + 1) * 128, :], in_=scrA[:]), 16)
                    block.sync(f_sd)

                def f_fin(sy):
                    sy.wait_ge(st.sem, st.v)
                block.sync(f_fin)
                return
            if DBG_XN:
                def f_vx(v):
                    wsnap(v, s0)
                    for g in range(G):
                        i = v.tensor_copy(scrA[:], XN[:, g, :])
                        ve.inc(i)
                block.vector(f_vx)

            def f_sy(sy):
                wsnap(sy, s0)
                if DBG_XN:
                    sy.wait_ge(ve.sem, ve.v)
                for g in range(G):
                    st.inc(sy.dma_start(out=y_d[g * 128:(g + 1) * 128, :], in_=XB[:, g, :]), 16)
                sy.wait_ge(st.sem, st.v)
            block.sync(f_sy)

        if nsub >= 1:
            emit_ln()
            if DBG_STOP != "ln":
                emit_attn_a()
        if nsub >= 2:
            emit_ln()
            emit_ffn()
        if nsub >= 3:
            emit_ln()
            emit_attn_b()
        if nsub >= 4:
            emit_ln()
            emit_ffn()
        emit_store()

    return nc


def kernel(**inputs):
    cst = host_consts()
    xs = np.ascontiguousarray(np.transpose(np.asarray(inputs["L0_attn_a_x"], np.float32), (0, 2, 1)))
    pkvT1 = np.asarray(inputs["L0_attn_a_past_kv"], np.float32).transpose(1, 0, 2).reshape(64, 1024)
    pkvT = np.ascontiguousarray(np.concatenate([pkvT1, pkvT1], axis=0))
    base = dict(
        wq=np.ascontiguousarray(inputs["L0_attn_a_Wq"], np.float32),
        wk=np.ascontiguousarray(inputs["L0_attn_a_Wk"], np.float32),
        wv=np.ascontiguousarray(inputs["L0_attn_a_Wv"], np.float32),
        pkvT=pkvT,
        gnw=np.ascontiguousarray(np.asarray(inputs["L0_attn_a_gn_w"], np.float32).reshape(H, 1)),
        gnb=np.ascontiguousarray(np.asarray(inputs["L0_attn_a_gn_b"], np.float32).reshape(H, 1)),
        wwT=np.ascontiguousarray(np.asarray(inputs["L0_attn_b_W_w"], np.float32).T),
        wb=np.ascontiguousarray(np.asarray(inputs["L0_attn_b_W_b"], np.float32).reshape(1, 3072)),
        owT=np.ascontiguousarray(np.asarray(inputs["L0_attn_b_out_w"], np.float32).T),
        ob=np.ascontiguousarray(np.asarray(inputs["L0_attn_b_out_b"], np.float32).reshape(1, 1024)),
        wr=np.ascontiguousarray(inputs["L0_ffn_Wr"], np.float32),
        wu=np.ascontiguousarray(inputs["L0_ffn_Wu"], np.float32),
        wg=np.ascontiguousarray(inputs["L0_ffn_Wg"], np.float32),
        wd=np.ascontiguousarray(inputs["L0_ffn_Wd"], np.float32),
        wus=np.ascontiguousarray(inputs["L0_ffn_Wu_s"], np.float32),
        wgs=np.ascontiguousarray(inputs["L0_ffn_Wg_s"], np.float32),
        wds=np.ascontiguousarray(inputs["L0_ffn_Wd_s"], np.float32),
        dmask=cst["dmask"], dmaskT=cst["dmaskT"], gq=cst["gq"], gc=cst["gc"],
        gk=cst["gk"], gcu=cst["gcu"], sel=cst["sel"], ident=cst["ident"], ones32=cst["ones32"],
    )
    nc = build_nc(NSUB)
    in_maps = [dict(base, x=np.ascontiguousarray(xs[b])) for b in range(8)]
    res = run_bass_kernel_spmd(nc, in_maps, core_ids=list(range(8)))
    out = np.stack([np.asarray(r["y"], np.float32).T for r in res.results], axis=0)
    return out


# version 09
# speedup vs baseline: 1.0209x; 1.0209x over previous
import numpy as np
import ml_dtypes
from contextlib import ExitStack

import concourse.bass as bass
import concourse.mybir as mybir
from concourse.bass_utils import run_bass_kernel_spmd

F32 = mybir.dt.float32
BF16 = mybir.dt.bfloat16
AF = mybir.ActivationFunctionType
ALU = mybir.AluOpType
AX = mybir.AxisListType.X

G = 8
T = 2048
TC = 512
NT = 4
CH = 128
NCH = 16
HP = 8
H, DH = 16, 64
DFF = 4096
SLAB = 512
NSLAB = DFF // SLAB
EPS = 1e-5
NTOK_GN = float(H * DH * T)

NSUB = 4
DBG_STOP = None
DBG_XN = False
DBG_DUMP = None


class Ctr:
    def __init__(self, sem):
        self.sem = sem
        self.v = 0

    def inc(self, inst, n=1):
        inst.then_inc(self.sem, n)
        self.v += n
        return self.v


def host_consts():
    gamma = (1.0 - 2.0 ** (-5.0 - np.arange(H, dtype=np.float32))).astype(np.float32)
    j = np.arange(CH, dtype=np.float32)
    diff = j[:, None] - j[None, :]
    dmask = np.where(diff[None] >= 0, gamma[:, None, None] ** np.maximum(diff[None], 0.0), 0.0).astype(np.float32)
    dmaskT = np.ascontiguousarray(np.transpose(dmask, (0, 2, 1)))
    gq = np.zeros((HP, 128, CH), np.float32)
    gc = np.zeros((HP, 128, CH), np.float32)
    gk = np.zeros((HP, 128, CH), np.float32)
    gcu = np.zeros((128, HP), np.float32)
    for hp in range(HP):
        for half in range(2):
            g_ = gamma[2 * hp + half]
            rows = slice(64 * half, 64 * half + 64)
            gq[hp, rows, :] = g_ ** j[None, :]
            gc[hp, rows, :] = g_ ** (CH - j[None, :])
            gk[hp, :, rows] = (g_ ** (CH - j[:, None]))
            gcu[rows, hp] = g_ ** CH
    sel = np.zeros((G, H, 128), np.float32)
    for g in range(G):
        for p in range(128):
            sel[g, (128 * g + p) // DH, p] = 1.0
    ident = np.eye(128, dtype=np.float32)
    ones32 = np.ones((1, TC), np.float32)
    return dict(dmask=dmask, dmaskT=dmaskT, gq=gq, gc=gc, gk=gk, gcu=gcu,
                sel=sel, ident=ident, ones32=ones32)


def build_nc(nsub=4):
    nc = bass.Bass("TRN2", target_bir_lowering=False)

    def din(name, shape):
        return nc.dram_tensor(name, list(shape), F32, kind="ExternalInput").ap()

    x_d = din("x", [1024, T])
    wq_d = din("wq", [1024, 1024])
    wk_d = din("wk", [1024, 1024])
    wv_d = din("wv", [1024, 1024])
    pkv_d = din("pkvT", [128, 1024])
    gnw_d = din("gnw", [H, 1])
    gnb_d = din("gnb", [H, 1])
    wwT_d = din("wwT", [1024, 3072])
    wb_d = din("wb", [1, 3072])
    owT_d = din("owT", [1024, 1024])
    ob_d = din("ob", [1, 1024])
    wr_d = din("wr", [1024, 8])
    wu_d = din("wu", [7, 1024, DFF])
    wg_d = din("wg", [7, 1024, DFF])
    wd_d = din("wd", [7, DFF, 1024])
    wus_d = din("wus", [1024, DFF])
    wgs_d = din("wgs", [1024, DFF])
    wds_d = din("wds", [DFF, 1024])
    dmask_d = din("dmask", [H, CH, CH])
    dmaskT_d = din("dmaskT", [H, CH, CH])
    gq_d = din("gq", [HP, 128, CH])
    gc_d = din("gc", [HP, 128, CH])
    gk_d = din("gk", [HP, 128, CH])
    gcu_d = din("gcu", [128, HP])
    sel_d = din("sel", [G, H, 128])
    ident_d = din("ident", [128, 128])
    ones32_d = din("ones32", [1, TC])
    y_d = nc.dram_tensor("y", [1024, T], F32, kind="ExternalOutput").ap()

    with ExitStack() as ctx:
        sb = lambda n, s, d: ctx.enter_context(nc.sbuf_tensor("sb_" + n, s, d))
        psm = lambda n, s, d: ctx.enter_context(nc.psum_tensor(n, s, d))
        sem = lambda n: ctx.enter_context(nc.semaphore(n))

        XB = sb("XB", [128, G, T], F32)
        XN = sb("XN", [128, G, T], BF16)
        FM = sb("FM", [128, G, T], BF16)
        id32 = sb("id32", [128, 128], F32)
        ones32 = sb("ones32", [1, TC], F32)
        onesbf = sb("onesbf", [1, TC], BF16)
        onescol = sb("onescol", [128, 1], BF16)
        onescol32 = sb("onescol32", [128, 1], F32)
        arow = sb("arow", [1, T], F32)
        brow = sb("brow", [1, T], F32)
        scrA = sb("scrA", [128, T], F32)
        vals = sb("vals", [1, 16], F32)
        gnw16 = sb("gnw16", [H, 1], F32)
        gnb16 = sb("gnb16", [H, 1], F32)
        rb16 = sb("rb16", [H, 2], F32)
        stat2 = sb("stat2", [H, 2], F32)
        stcol = sb("stcol", [128, 2 * G], F32)
        wr32 = sb("wr32", [128, 8, 8], F32)
        wrsrow = sb("wrsrow", [1, 8], F32)
        wrsbc = sb("wrsbc", [128, 8], F32)
        epsc = sb("epsc", [1, 1], F32)

        P1 = psm("P1", [128, T], F32)
        P2 = psm("P2", [128, T], F32)

        ld = Ctr(sem("s_ld"))
        gw = Ctr(sem("s_gw"))
        pe = Ctr(sem("s_pe"))
        ve = Ctr(sem("s_ve"))
        se = Ctr(sem("s_se"))
        gp = Ctr(sem("s_gp"))
        st = Ctr(sem("s_st"))
        ALLC = dict(ld=ld, gw=gw, pe=pe, ve=ve, se=se, gp=gp, st=st)

        block = ctx.enter_context(nc.Block())

        phase_id = [0]

        def snap():
            return {k: c.v for k, c in ALLC.items()}

        def wsnap(eng, s):
            for k, v in s.items():
                if v > 0:
                    eng.wait_ge(ALLC[k].sem, v)

        def f_sync0(sy):
            ld.inc(sy.dma_start(out=XB[:], in_=x_d.rearrange("(g p) t -> p g t", p=128)), 16)
            ld.inc(sy.dma_start(out=id32[:], in_=ident_d), 16)
            ld.inc(sy.dma_start(out=ones32[:], in_=ones32_d), 16)
            ld.inc(sy.dma_start(out=gnw16[:], in_=gnw_d), 16)
            ld.inc(sy.dma_start(out=gnb16[:], in_=gnb_d), 16)
            ld.inc(sy.dma_start(out=wr32[:], in_=wr_d.rearrange("(k p) e -> p k e", p=128)), 16)
        block.sync(f_sync0)

        def f_g0(g_):
            gw.inc(g_.dma_start(out=onesbf[:], in_=ones32_d), 16)
            gp.inc(g_.memset(onescol[:], 1.0), 1)
            gp.inc(g_.memset(onescol32[:], 1.0), 1)
            gp.inc(g_.memset(epsc[:], EPS), 1)
        block.gpsimd(f_g0)

        def emit_ln():
            s0 = snap()
            phase_id[0] += 1
            pid = str(phase_id[0])
            with nc.sbuf_tensor("ln_sum" + pid, [1, T], F32) as sumrow, \
                 nc.sbuf_tensor("ln_sq" + pid, [1, T], F32) as sqrow, \
                 nc.sbuf_tensor("ln_tmp" + pid, [1, T], F32) as tmprow:

                def f_s(se_):
                    wsnap(se_, s0)
                    for g in range(G):
                        i = se_.activation(XN[:, g, :], XB[:, g, :], AF.Copy)
                    for g in range(G):
                        i = se_.activation(FM[:, g, :], XB[:, g, :], AF.Square)
                    se.inc(i)
                block.scalar(f_s)
                se_ready = se.v

                def f_t(te):
                    wsnap(te, s0)
                    te.wait_ge(se.sem, se_ready)
                    for t in range(NT):
                        for g in range(G):
                            i = te.matmul(P1[0:1, t * TC:(t + 1) * TC], onescol[:],
                                          XN[:, g, t * TC:(t + 1) * TC], start=(g == 0), stop=(g == G - 1))
                        pe.inc(i)
                        for g in range(G):
                            i = te.matmul(P2[0:1, t * TC:(t + 1) * TC], onescol[:],
                                          FM[:, g, t * TC:(t + 1) * TC], start=(g == 0), stop=(g == G - 1))
                        pe.inc(i)
                block.tensor(f_t)
                pe_stats = pe.v

                def f_v(v):
                    v.wait_ge(pe.sem, pe_stats)
                    ve.inc(v.tensor_scalar(sumrow[:], P1[0:1, :], 1.0 / 1024.0, None, ALU.mult))
                    ve.inc(v.tensor_scalar(sqrow[:], P2[0:1, :], 1.0 / 1024.0, None, ALU.mult))
                    v.wait_ge(ve.sem, ve.v)
                    ve.inc(v.tensor_tensor(tmprow[:], sumrow[:], sumrow[:], ALU.mult))
                    v.wait_ge(ve.sem, ve.v)
                    ve.inc(v.tensor_tensor(sqrow[:], sqrow[:], tmprow[:], ALU.subtract))
                block.vector(f_v)
                ve_var = ve.v

                def f_s2(se_):
                    se_.wait_ge(ve.sem, ve_var)
                    se.inc(se_.activation(tmprow[:], sqrow[:], AF.Sqrt, bias=epsc[0:1, 0:1]))
                block.scalar(f_s2)
                se_sqrt = se.v

                def f_v2(v):
                    v.wait_ge(se.sem, se_sqrt)
                    ve.inc(v.reciprocal(arow[:], tmprow[:]))
                    v.wait_ge(ve.sem, ve.v)
                    ve.inc(v.scalar_tensor_tensor(brow[:], sumrow[:], -1.0, arow[:], ALU.mult, ALU.mult))
                block.vector(f_v2)
                ve_ab = ve.v

                def f_t2(te):
                    te.wait_ge(ve.sem, ve_ab)
                    for t in range(NT):
                        pe.inc(te.matmul(P1[:, t * TC:(t + 1) * TC], ones32[0:1, 0:128],
                                         arow[0:1, t * TC:(t + 1) * TC], start=True, stop=True))
                        pe.inc(te.matmul(P2[:, t * TC:(t + 1) * TC], ones32[0:1, 0:128],
                                         brow[0:1, t * TC:(t + 1) * TC], start=True, stop=True))
                block.tensor(f_t2)
                pe_bc = pe.v

                def f_v3(v):
                    v.wait_ge(pe.sem, pe_bc)
                    for g in range(G):
                        v.tensor_tensor(XN[:, g, :], XN[:, g, :], P1[:, :], ALU.mult)
                        i = v.tensor_tensor(XN[:, g, :], XN[:, g, :], P2[:, :], ALU.add)
                    ve.inc(i)
                block.vector(f_v3)

        def emit_attn_a():
            s0 = snap()
            with ExitStack() as actx:
                phase_id[0] += 1
                asb = lambda n, s, d: actx.enter_context(nc.sbuf_tensor(n + str(phase_id[0]), s, d))
                dmT = asb("dmT", [128, H, CH], BF16)
                dmv = asb("dmv", [128, H, CH], BF16)
                gqs = asb("gqs", [128, HP, CH], BF16)
                gcs = asb("gcs", [128, HP, CH], BF16)
                gks = asb("gks", [128, HP, CH], BF16)
                gcus = asb("gcus", [128, HP], F32)
                sel32 = asb("sel32", [H, G, 128], F32)
                pkv = asb("pkv", [128, 1024], BF16)
                wqb = asb("wqb", [128, 8, 128], BF16)
                wkb = asb("wkb", [128, 8, 128], BF16)
                wvb = asb("wvb", [128, 8, 128], BF16)
                qf = asb("qf", [128, T], BF16)
                kf = asb("kf", [128, T], BF16)
                kt = asb("kt", [128, T], BF16)
                vt = asb("vt", [128, T], BF16)
                ppt = asb("ppt", [128, T], BF16)
                qt1 = asb("qt1", [128, CH], BF16)
                ab1 = asb("ab1", [128, CH], BF16)
                ab2_ = asb("ab2_", [128, CH], BF16)
                tmc = asb("tmc", [128, CH], F32)
                carry = asb("carry", [128, 1], F32)
                N32 = asb("N32", [128, DH], F32)
                Nbf = asb("Nbf", [128, DH], BF16)

                def f_g(g_):
                    wsnap(g_, s0)
                    gw.inc(g_.dma_start(out=dmT[:], in_=dmaskT_d.rearrange("h s t -> s h t")), 16)
                    gw.inc(g_.dma_start(out=dmv[:], in_=dmask_d.rearrange("h s t -> s h t")), 16)
                    gw.inc(g_.dma_start(out=gqs[:], in_=gq_d.rearrange("h p c -> p h c")), 16)
                    gw.inc(g_.dma_start(out=gcs[:], in_=gc_d.rearrange("h p c -> p h c")), 16)
                    gw.inc(g_.dma_start(out=gks[:], in_=gk_d.rearrange("h p c -> p h c")), 16)
                    gw.inc(g_.dma_start(out=pkv[:], in_=pkv_d), 16)
                    gw.inc(g_.dma_start(out=wqb[:], in_=wq_d[:, 0:128].rearrange("(a p) c -> p a c", p=128)), 16)
                    gw.inc(g_.dma_start(out=wkb[:], in_=wk_d[:, 0:128].rearrange("(a p) c -> p a c", p=128)), 16)
                    gw.inc(g_.dma_start(out=wvb[:], in_=wv_d[:, 0:128].rearrange("(a p) c -> p a c", p=128)), 16)
                block.gpsimd(f_g)
                gw_consts = gw.v

                def f_sy(sy):
                    wsnap(sy, s0)
                    ld.inc(sy.dma_start(out=gcus[:], in_=gcu_d), 16)
                    ld.inc(sy.dma_start(out=sel32[:], in_=sel_d.rearrange("g h c -> h g c")), 16)
                block.sync(f_sy)
                ld_consts = ld.v

                gw_w = gw_consts
                for hp in range(HP):
                    hA, hB = 2 * hp, 2 * hp + 1
                    ve_prev = ve.v

                    def f_t1(te, hp=hp, gw_w=gw_w, ve_prev=ve_prev):
                        te.wait_ge(gw.sem, gw_w)
                        te.wait_ge(ve.sem, ve_prev)
                        if hp == 0:
                            wsnap(te, s0)
                            te.wait_ge(ld.sem, ld_consts)
                        for t in range(NT):
                            for k in range(G):
                                i = te.matmul(P1[:, t * TC:(t + 1) * TC], wqb[:, k, :],
                                              XN[:, k, t * TC:(t + 1) * TC], start=(k == 0), stop=(k == G - 1))
                            pe.inc(i)
                            for k in range(G):
                                i = te.matmul(P2[:, t * TC:(t + 1) * TC], wkb[:, k, :],
                                              XN[:, k, t * TC:(t + 1) * TC], start=(k == 0), stop=(k == G - 1))
                            pe.inc(i)
                    block.tensor(f_t1)
                    pe_qk = pe.v

                    def f_v1(v, pe_qk=pe_qk):
                        v.wait_ge(pe.sem, pe_qk)
                        ve.inc(v.tensor_copy(qf[:], P1[:, :]))
                        ve.inc(v.tensor_copy(kf[:], P2[:, :]))
                    block.vector(f_v1)
                    ve_qk = ve.v

                    def f_t2(te, ve_qk=ve_qk):
                        te.wait_ge(ve.sem, ve_qk)
                        for tt in range(NCH):
                            for k in range(G):
                                i = te.matmul(P1[:, tt * CH:(tt + 1) * CH], XN[:, k, tt * CH:(tt + 1) * CH],
                                              wkb[:, k, :], start=(k == 0), stop=(k == G - 1))
                            pe.inc(i)
                            for k in range(G):
                                i = te.matmul(P2[:, tt * CH:(tt + 1) * CH], XN[:, k, tt * CH:(tt + 1) * CH],
                                              wvb[:, k, :], start=(k == 0), stop=(k == G - 1))
                            pe.inc(i)
                    block.tensor(f_t2)
                    pe_kv = pe.v

                    def f_v2(v, pe_kv=pe_kv):
                        v.wait_ge(pe.sem, pe_kv)
                        ve.inc(v.tensor_copy(kt[:], P1[:, :]))
                        ve.inc(v.tensor_copy(vt[:], P2[:, :]))
                    block.vector(f_v2)
                    ve_kv = ve.v

                    if hp + 1 < HP:
                        def f_gp(g_, hp=hp, pe_kv=pe_kv):
                            g_.wait_ge(pe.sem, pe_kv)
                            c0 = 128 * (hp + 1)
                            gw.inc(g_.dma_start(out=wqb[:], in_=wq_d[:, c0:c0 + 128].rearrange("(a p) c -> p a c", p=128)), 16)
                            gw.inc(g_.dma_start(out=wkb[:], in_=wk_d[:, c0:c0 + 128].rearrange("(a p) c -> p a c", p=128)), 16)
                            gw.inc(g_.dma_start(out=wvb[:], in_=wv_d[:, c0:c0 + 128].rearrange("(a p) c -> p a c", p=128)), 16)
                        block.gpsimd(f_gp)
                        gw_w = gw.v

                    if DBG_STOP == "aa_proj":
                        continue

                    def f_t3(te, ve_kv=ve_kv, hA=hA, hB=hB):
                        te.wait_ge(ve.sem, ve_kv)
                        for I in range(NCH):
                            te.matmul(P1[:, I * 64:(I + 1) * 64], qf[0:64, I * CH:(I + 1) * CH],
                                      pkv[0:64, hA * 64:hA * 64 + 64], start=True, stop=True)
                            i = te.matmul(P1[:, 1024 + I * 64:1024 + (I + 1) * 64], qf[64:128, I * CH:(I + 1) * CH],
                                          pkv[64:128, hB * 64:hB * 64 + 64], start=True, stop=True)
                        pe.inc(i)
                    block.tensor(f_t3)
                    pe_pp = pe.v

                    def f_v3(v, pe_pp=pe_pp, hp=hp):
                        v.wait_ge(pe.sem, pe_pp)
                        pp4 = ppt.ap().rearrange("p (c h d) -> p c h d", h=2, d=64)
                        v.tensor_copy(pp4[:, :, 0, :], P1.ap().rearrange("p (x c d) -> p x c d", x=2, d=64)[:, 0, :, :])
                        ve.inc(v.tensor_copy(pp4[:, :, 1, :], P1.ap().rearrange("p (x c d) -> p x c d", x=2, d=64)[:, 1, :, :]))
                        ve.inc(v.memset(carry[:], 0.0))
                        if DBG_DUMP == "scrA" and DBG_STOP == "aa_pp" and hp == HP - 1:
                            v.wait_ge(ve.sem, ve.v)
                            ve.inc(v.tensor_copy(scrA[:], ppt[:]))
                        if DBG_DUMP == "pkv" and hp == HP - 1:
                            v.wait_ge(ve.sem, ve.v)
                            ve.inc(v.tensor_copy(scrA[:, 0:1024], pkv[:]))
                            ve.inc(v.tensor_copy(scrA[:, 1024:2048], qf[:, 0:1024]))
                    block.vector(f_v3)
                    ve_pp = ve.v

                    if DBG_STOP == "aa_pp":
                        continue

                    def f_t4(te, ve_pp=ve_pp, hA=hA, hB=hB):
                        te.wait_ge(ve.sem, ve_pp)
                        for I in range(NCH - 1, -1, -1):
                            te.matmul(P2[0:64, I * CH:(I + 1) * CH], ppt[:, I * CH:I * CH + 64],
                                      dmv[:, hA, :], start=True, stop=True)
                            pe.inc(te.matmul(P2[64:128, I * CH:(I + 1) * CH], ppt[:, I * CH + 64:(I + 1) * CH],
                                             dmv[:, hB, :], start=True, stop=True))
                    block.tensor(f_t4)
                    pe_scan0 = pe.v - NCH

                    if DBG_STOP == "aa_scan_t":
                        def f_v4x(v, pe_scan0=pe_scan0):
                            v.wait_ge(pe.sem, pe_scan0 + NCH)
                            ve.inc(v.tensor_copy(scrA[:], P2[:, :]))
                        block.vector(f_v4x)
                        continue

                    def f_v4(v, pe_scan0=pe_scan0, hp=hp):
                        v.wait_ge(pe.sem, pe_scan0 + NCH)
                        for idx, I in enumerate(range(NCH - 1, -1, -1)):
                            v.wait_ge(ve.sem, ve.v)
                            v.tensor_scalar(tmc[:], gcs[:, hp, :], carry[:, 0:1], None, ALU.mult)
                            ve.inc(v.tensor_tensor(scrA[:, I * CH:(I + 1) * CH], P2[:, I * CH:(I + 1) * CH], tmc[:], ALU.add))
                            v.wait_ge(ve.sem, ve.v)
                            ve.inc(v.tensor_copy(carry[:], scrA[:, I * CH:I * CH + 1]))
                    block.vector(f_v4)

                    pe_allscan = pe.v
                    if DBG_STOP == "aa_scan_nokhat":
                        continue

                    def f_v5(v, pe_allscan=pe_allscan, hp=hp):
                        v.wait_ge(pe.sem, pe_allscan)
                        for I in range(NCH):
                            i = v.tensor_tensor(ppt[:, I * CH:(I + 1) * CH], kt[:, I * CH:(I + 1) * CH],
                                                gks[:, hp, :], ALU.mult)
                        ve.inc(i)
                        v.memset(N32[:], 0.0)
                        ve.inc(v.memset(Nbf[:], 0.0))
                    block.vector(f_v5)
                    ve_khat = ve.v

                    if DBG_STOP == "aa_scan":
                        continue
                    pe_sc = {}
                    ve_ab_ = {}
                    ve_ret = {}
                    ve_nup = {}
                    for I in range(NCH):
                        def f_te(te, I=I, hp=hp):
                            if I == 0:
                                te.wait_ge(ve.sem, ve_khat)
                            else:
                                te.wait_ge(ve.sem, ve_ab_[I - 1])
                            te.matmul(P1[:, 0:128], kf[0:64, I * CH:(I + 1) * CH],
                                      qf[0:64, I * CH:(I + 1) * CH], start=True, stop=True)
                            pe.inc(te.matmul(P2[:, 0:128], kf[64:128, I * CH:(I + 1) * CH],
                                             qf[64:128, I * CH:(I + 1) * CH], start=True, stop=True))
                            pe_sc[I] = pe.v
                        block.tensor(f_te)

                        def f_ve(v, I=I, hp=hp):
                            v.wait_ge(pe.sem, pe_sc[I])
                            v.tensor_tensor(ab1[:], P1[:, 0:128], dmT[:, 2 * hp, :], ALU.mult)
                            v.tensor_tensor(ab2_[:], P2[:, 0:128], dmT[:, 2 * hp + 1, :], ALU.mult)
                            ve.inc(v.tensor_tensor(qt1[:], qf[:, I * CH:(I + 1) * CH], gqs[:, hp, :], ALU.mult))
                            ve_ab_[I] = ve.v
                        block.vector(f_ve)

                        def f_te2(te, I=I):
                            te.wait_ge(ve.sem, ve_ab_[I])
                            if I >= 1:
                                te.wait_ge(ve.sem, ve_ret[I - 1])
                            te.matmul(P1[0:64, 512:640], vt[:, I * CH:I * CH + 64], ab1[:], start=True, stop=False)
                            te.matmul(P1[0:64, 512:640], Nbf[0:64, :], qt1[0:64, :], start=False, stop=True)
                            te.matmul(P1[64:128, 512:640], vt[:, I * CH + 64:(I + 1) * CH], ab2_[:], start=True, stop=False)
                            pe.inc(te.matmul(P1[64:128, 512:640], Nbf[64:128, :], qt1[64:128, :], start=False, stop=True))
                            pe_sc[(I, 'r')] = pe.v
                            nb = 1024 if I % 2 == 0 else 1536
                            if I >= 2:
                                te.wait_ge(ve.sem, ve_nup[I - 2])
                            te.matmul(P1[0:64, nb:nb + 64], ppt[:, I * CH:I * CH + 64],
                                      vt[:, I * CH:I * CH + 64], start=True, stop=True)
                            pe.inc(te.matmul(P1[64:128, nb:nb + 64], ppt[:, I * CH + 64:(I + 1) * CH],
                                             vt[:, I * CH + 64:(I + 1) * CH], start=True, stop=True))
                            pe_sc[(I, 'n')] = pe.v
                        block.tensor(f_te2)

                        def f_ve2(v, I=I, hp=hp):
                            v.wait_ge(pe.sem, pe_sc[(I, 'r')])
                            ve.inc(v.tensor_tensor(FM[:, hp, I * CH:(I + 1) * CH], P1[:, 512:640],
                                                   scrA[:, I * CH:(I + 1) * CH], ALU.add))
                            ve_ret[I] = ve.v
                            nb = 1024 if I % 2 == 0 else 1536
                            v.wait_ge(pe.sem, pe_sc[(I, 'n')])
                            v.scalar_tensor_tensor(N32[:], N32[:], gcus[:, hp:hp + 1], P1[:, nb:nb + 64], ALU.mult, ALU.add)
                            v.wait_ge(ve.sem, ve.v)
                            ve.inc(v.tensor_copy(Nbf[:], N32[:]))
                            ve_nup[I] = ve.v
                        block.vector(f_ve2)

                if DBG_STOP == "aa_fwd":
                    return
                sA = snap()

                def f_s3(se_):
                    wsnap(se_, sA)
                    for g in range(G):
                        i = se_.activation(XN[:, g, :], FM[:, g, :], AF.Square)
                    se.inc(i)
                block.scalar(f_s3)
                se_sq = se.v

                def f_t5(te):
                    te.wait_ge(se.sem, se_sq)
                    first = True
                    for g in range(G):
                        for t in range(NT):
                            i = te.matmul(P1[0:1, 0:TC], onescol[:], FM[:, g, t * TC:(t + 1) * TC],
                                          start=first, stop=(g == G - 1 and t == NT - 1))
                            first = False
                    pe.inc(i)
                    first = True
                    for g in range(G):
                        for t in range(NT):
                            i = te.matmul(P2[0:1, 0:TC], onescol[:], XN[:, g, t * TC:(t + 1) * TC],
                                          start=first, stop=(g == G - 1 and t == NT - 1))
                            first = False
                    pe.inc(i)
                block.tensor(f_t5)
                pe_gn = pe.v

                def f_v6(v):
                    v.wait_ge(pe.sem, pe_gn)
                    ve.inc(v.reduce_sum(vals[0:1, 0:1], P1[0:1, 0:TC], axis=AX))
                    ve.inc(v.reduce_sum(vals[0:1, 1:2], P2[0:1, 0:TC], axis=AX))
                    v.wait_ge(ve.sem, ve.v)
                    v.tensor_scalar(vals[0:1, 2:3], vals[0:1, 0:1], 1.0 / NTOK_GN, None, ALU.mult)
                    ve.inc(v.tensor_scalar(vals[0:1, 3:4], vals[0:1, 1:2], 1.0 / NTOK_GN, None, ALU.mult))
                    v.wait_ge(ve.sem, ve.v)
                    ve.inc(v.tensor_tensor(vals[0:1, 4:5], vals[0:1, 2:3], vals[0:1, 2:3], ALU.mult))
                    v.wait_ge(ve.sem, ve.v)
                    ve.inc(v.tensor_tensor(vals[0:1, 5:6], vals[0:1, 3:4], vals[0:1, 4:5], ALU.subtract))
                block.vector(f_v6)
                ve_gnv = ve.v

                def f_s4(se_):
                    se_.wait_ge(ve.sem, ve_gnv)
                    se.inc(se_.activation(vals[0:1, 6:7], vals[0:1, 5:6], AF.Sqrt, bias=epsc[0:1, 0:1]))
                block.scalar(f_s4)
                se_gns = se.v

                def f_v7(v):
                    v.wait_ge(se.sem, se_gns)
                    ve.inc(v.reciprocal(vals[0:1, 8:9], vals[0:1, 6:7]))
                    v.wait_ge(ve.sem, ve.v)
                    ve.inc(v.tensor_copy(vals[0:1, 9:10], vals[0:1, 2:3]))
                block.vector(f_v7)
                ve_gnr = ve.v

                def f_t6(te):
                    te.wait_ge(ve.sem, ve_gnr)
                    pe.inc(te.matmul(P1[0:16, 0:2], ones32[0:1, 0:16], vals[0:1, 8:10], start=True, stop=True))
                block.tensor(f_t6)
                pe_b16 = pe.v

                def f_v8(v):
                    v.wait_ge(pe.sem, pe_b16)
                    ve.inc(v.tensor_copy(rb16[:], P1[0:16, 0:2]))
                    v.wait_ge(ve.sem, ve.v)
                    ve.inc(v.tensor_tensor(stat2[:, 0:1], gnw16[:], rb16[:, 0:1], ALU.mult))
                    v.wait_ge(ve.sem, ve.v)
                    ve.inc(v.tensor_tensor(stat2[:, 1:2], stat2[:, 0:1], rb16[:, 1:2], ALU.mult))
                    v.wait_ge(ve.sem, ve.v)
                    ve.inc(v.tensor_tensor(stat2[:, 1:2], gnb16[:], stat2[:, 1:2], ALU.subtract))
                block.vector(f_v8)
                ve_st2 = ve.v

                def f_t7(te):
                    te.wait_ge(ve.sem, ve_st2)
                    for g in range(G):
                        i = te.matmul(P2[:, 2 * g:2 * g + 2], sel32[:, g, :], stat2[:], start=True, stop=True)
                    pe.inc(i)
                block.tensor(f_t7)
                pe_sel = pe.v

                def f_v9(v):
                    v.wait_ge(pe.sem, pe_sel)
                    ve.inc(v.tensor_copy(stcol[:], P2[:, 0:2 * G]))
                    v.wait_ge(ve.sem, ve.v)
                    for g in range(G):
                        v.scalar_tensor_tensor(XB[:, g, :], FM[:, g, :], stcol[:, 2 * g:2 * g + 1],
                                               XB[:, g, :], ALU.mult, ALU.add)
                        i = v.tensor_scalar(XB[:, g, :], XB[:, g, :], stcol[:, 2 * g + 1:2 * g + 2], None, ALU.add)
                    ve.inc(i)
                block.vector(f_v9)

        def emit_attn_b():
            s0 = snap()
            with ExitStack() as actx:
                phase_id[0] += 1
                asb = lambda n, s, d: actx.enter_context(nc.sbuf_tensor(n + str(phase_id[0]), s, d))
                wqb = asb("bwq", [128, 8, 128], BF16)
                wkb = asb("bwk", [128, 8, 128], BF16)
                wvb = asb("bwv", [128, 8, 128], BF16)
                owT = asb("owT", [128, 8, 1024], BF16)
                wbq = asb("wbq", [1, 3072], BF16)
                obq = asb("obq", [1, 1024], BF16)
                qf = asb("bqf", [128, T], BF16)
                kf = asb("bkf", [128, T], BF16)
                vt = asb("bvt", [128, NCH, 130], BF16)
                eb = asb("eb", [128, T], BF16)
                zrow = asb("zrow", [1, TC], F32)
                zb = asb("zb", [64, TC], F32)

                def f_g(g_):
                    wsnap(g_, s0)
                    gw.inc(g_.dma_start(out=wbq[:], in_=wb_d), 16)
                    gw.inc(g_.dma_start(out=obq[:], in_=ob_d), 16)
                    gw.inc(g_.dma_start(out=wqb[:], in_=wwT_d[:, 0:128].rearrange("(a p) c -> p a c", p=128)), 16)
                    gw.inc(g_.dma_start(out=wkb[:], in_=wwT_d[:, 1024:1152].rearrange("(a p) c -> p a c", p=128)), 16)
                    gw.inc(g_.dma_start(out=wvb[:], in_=wwT_d[:, 2048:2176].rearrange("(a p) c -> p a c", p=128)), 16)
                    gw.inc(g_.dma_start(out=owT[:], in_=owT_d.rearrange("(a p) c -> p a c", p=128)), 16)
                    gp.inc(g_.memset(vt.ap().rearrange("p t (h c) -> p t h c", c=65)[:, :, :, 64:65], 1.0), 1)
                block.gpsimd(f_g)
                gw_w = gw.v
                gp_ones = gp.v

                for hp in range(HP):
                    ve_prev = ve.v

                    def f_t1(te, hp=hp, gw_w=gw_w, ve_prev=ve_prev):
                        te.wait_ge(gw.sem, gw_w)
                        te.wait_ge(ve.sem, ve_prev)
                        if hp == 0:
                            wsnap(te, s0)
                        for t in range(NT):
                            for k in range(G):
                                te.matmul(P1[:, t * TC:(t + 1) * TC], wqb[:, k, :],
                                          XN[:, k, t * TC:(t + 1) * TC], start=(k == 0), stop=False)
                            pe.inc(te.matmul(P1[:, t * TC:(t + 1) * TC], wbq[0:1, 128 * hp:128 * hp + 128],
                                             onesbf[0:1, 0:TC], start=False, stop=True))
                            for k in range(G):
                                te.matmul(P2[:, t * TC:(t + 1) * TC], wkb[:, k, :],
                                          XN[:, k, t * TC:(t + 1) * TC], start=(k == 0), stop=False)
                            pe.inc(te.matmul(P2[:, t * TC:(t + 1) * TC], wbq[0:1, 1024 + 128 * hp:1024 + 128 * hp + 128],
                                             onesbf[0:1, 0:TC], start=False, stop=True))
                    block.tensor(f_t1)
                    pe_qk = pe.v

                    def f_v1(v, pe_qk=pe_qk):
                        v.wait_ge(pe.sem, pe_qk)
                        ve.inc(v.tensor_copy(qf[:], P1[:, :]))
                        ve.inc(v.tensor_copy(kf[:], P2[:, :]))
                    block.vector(f_v1)
                    ve_qk = ve.v

                    def f_t2(te, hp=hp, ve_qk=ve_qk):
                        te.wait_ge(ve.sem, ve_qk)
                        if hp == 0:
                            te.wait_ge(gp.sem, gp_ones)
                        for tt in range(NCH):
                            for k in range(G):
                                te.matmul(P1[:, tt * CH:(tt + 1) * CH], XN[:, k, tt * CH:(tt + 1) * CH],
                                          wvb[:, k, :], start=(k == 0), stop=False)
                            i = te.matmul(P1[:, tt * CH:(tt + 1) * CH], onesbf[0:1, 0:128],
                                          wbq[0:1, 2048 + 128 * hp:2048 + 128 * hp + 128], start=False, stop=True)
                            pe.inc(i)
                    block.tensor(f_t2)
                    pe_v = pe.v

                    def f_v2(v, pe_v=pe_v):
                        v.wait_ge(pe.sem, pe_v)
                        p3 = P1.ap().rearrange("p (t c) -> p t c", c=CH)
                        v.tensor_copy(vt[:, :, 0:64], p3[:, :, 0:64])
                        ve.inc(v.tensor_copy(vt[:, :, 65:129], p3[:, :, 64:128]))
                    block.vector(f_v2)
                    ve_v = ve.v

                    if hp + 1 < HP:
                        def f_gp(g_, hp=hp, pe_v=pe_v):
                            g_.wait_ge(pe.sem, pe_v)
                            c0 = 128 * (hp + 1)
                            gw.inc(g_.dma_start(out=wqb[:], in_=wwT_d[:, c0:c0 + 128].rearrange("(a p) c -> p a c", p=128)), 16)
                            gw.inc(g_.dma_start(out=wkb[:], in_=wwT_d[:, 1024 + c0:1024 + c0 + 128].rearrange("(a p) c -> p a c", p=128)), 16)
                            gw.inc(g_.dma_start(out=wvb[:], in_=wwT_d[:, 2048 + c0:2048 + c0 + 128].rearrange("(a p) c -> p a c", p=128)), 16)
                        block.gpsimd(f_gp)
                        gw_w = gw.v

                    if hp == 0:
                        norm_hist = []
                        exp_hist = []
                    for hloc in range(2):
                        hbase = 64 * hloc
                        voff = 65 * hloc
                        for t in range(NT):
                            ob = 0 if ((2 * hloc + t) % 2 == 0) else 512
                            se_exp = {}
                            pe_av = {}
                            last_norm = norm_hist[-2] if len(norm_hist) >= 2 else 0
                            last_exp = exp_hist[-1] if exp_hist else 0
                            for sb_ in range(4):
                                def f_ts(te, sb_=sb_, hbase=hbase, t=t, ve_v=ve_v, last_norm=last_norm, last_exp=last_exp):
                                    if sb_ == 0:
                                        te.wait_ge(ve.sem, max(ve_v, last_norm))
                                        te.wait_ge(se.sem, last_exp)
                                    if sb_ >= 1:
                                        te.wait_ge(se.sem, se_exp[sb_ - 1])
                                    for si in range(4):
                                        s = sb_ * 4 + si
                                        i = te.matmul(P1[:, si * TC:(si + 1) * TC],
                                                      kf[hbase:hbase + 64, s * CH:(s + 1) * CH],
                                                      qf[hbase:hbase + 64, t * TC:(t + 1) * TC], start=True, stop=True)
                                    pe.inc(i)
                                    pe_av[('s', sb_)] = pe.v
                                block.tensor(f_ts)

                                def f_se(se_, sb_=sb_):
                                    se_.wait_ge(pe.sem, pe_av[('s', sb_)])
                                    if sb_ >= 1:
                                        se_.wait_ge(pe.sem, pe_av[('a', sb_ - 1)])
                                    se.inc(se_.activation(eb[:], P1[:, :], AF.Exp, scale=0.125))
                                    se_exp[sb_] = se.v
                                block.scalar(f_se)

                                def f_ta(te, sb_=sb_, voff=voff, ob=ob):
                                    te.wait_ge(se.sem, se_exp[sb_])
                                    for si in range(4):
                                        s = sb_ * 4 + si
                                        i = te.matmul(P2[0:65, ob:ob + TC], vt[:, s, voff:voff + 65],
                                                      eb[:, si * TC:(si + 1) * TC],
                                                      start=(sb_ == 0 and si == 0), stop=(sb_ == 3 and si == 3))
                                    pe.inc(i)
                                    pe_av[('a', sb_)] = pe.v
                                block.tensor(f_ta)

                            def f_vn(v, ob=ob):
                                v.wait_ge(pe.sem, pe_av[('a', 3)])
                                ve.inc(v.reciprocal(zrow[:], P2[64:65, ob:ob + TC]))
                            block.vector(f_vn)
                            ve_r = ve.v

                            def f_tz(te, ve_r=ve_r):
                                te.wait_ge(ve.sem, ve_r)
                                pe.inc(te.matmul(P2[0:64, 1024:1024 + TC], ones32[0:1, 0:64],
                                                 zrow[0:1, 0:TC], start=True, stop=True))
                            block.tensor(f_tz)
                            pe_z = pe.v

                            def f_vz(v, hp=hp, hbase=hbase, t=t, ob=ob, pe_z=pe_z):
                                v.wait_ge(pe.sem, pe_z)
                                ve.inc(v.tensor_copy(zb[:], P2[0:64, 1024:1024 + TC]))
                                v.wait_ge(ve.sem, ve.v)
                                ve.inc(v.tensor_tensor(FM[hbase:hbase + 64, hp, t * TC:(t + 1) * TC],
                                                       P2[0:64, ob:ob + TC], zb[:], ALU.mult))
                            block.vector(f_vz)
                            norm_hist.append(ve.v)
                            exp_hist.append(se.v)

                ve_out = {}
                for m in range(G):
                    ve_prev2 = ve.v

                    def f_t8(te, m=m, ve_prev2=ve_prev2):
                        if m < 2:
                            te.wait_ge(ve.sem, ve_prev2)
                        else:
                            te.wait_ge(ve.sem, ve_out[m - 2])
                        PP = P1 if m % 2 == 0 else P2
                        for t in range(NT):
                            for k in range(G):
                                te.matmul(PP[:, t * TC:(t + 1) * TC], owT[:, k, 128 * m:128 * m + 128],
                                          FM[:, k, t * TC:(t + 1) * TC], start=(k == 0), stop=False)
                            i = te.matmul(PP[:, t * TC:(t + 1) * TC], obq[0:1, 128 * m:128 * m + 128],
                                          onesbf[0:1, 0:TC], start=False, stop=True)
                        pe.inc(i)
                    block.tensor(f_t8)
                    pe_out = pe.v

                    def f_v10(v, m=m, pe_out=pe_out):
                        v.wait_ge(pe.sem, pe_out)
                        PP = P1 if m % 2 == 0 else P2
                        ve.inc(v.tensor_tensor(XB[:, m, :], XB[:, m, :], PP[:, :], ALU.add))
                        ve_out[m] = ve.v
                    block.vector(f_v10)

        def emit_ffn():
            s0 = snap()
            with ExitStack() as fctx:
                phase_id[0] += 1
                fsb = lambda n, s, d: fctx.enter_context(nc.sbuf_tensor(n + str(phase_id[0]), s, d))
                wub = fsb("wub", [128, 8, SLAB], BF16)
                wgb = fsb("wgb", [128, 8, SLAB], BF16)
                wdb = fsb("wdb", [128, 4, 1024], BF16)
                LG = fsb("LG", [128, 128], F32)
                SBB = fsb("SBB", [128, 128], F32)
                et = fsb("et", [128, 128], F32)
                mk = fsb("mk", [128, 128], F32)
                pa = fsb("pa", [128, 128], F32)
                pb = fsb("pb", [128, 128], F32)
                ctm = fsb("ctm", [128, 128], F32)
                lgt = fsb("lgt", [128, 128], F32)
                abt_a = fsb("abt_a", [128, 16], F32)
                abt_b = fsb("abt_b", [128, 16], F32)
                mx16 = fsb("mx16", [128, 16], F32)
                z16 = fsb("z16", [128, 16], F32)
                c16 = fsb("c16", [128, 16], F32)
                cbrow = fsb("cbrow", [1, T], F32)
                scrB = fsb("scrB", [128, T], F32)

                def f_t1(te):
                    wsnap(te, s0)
                    for t in range(NT):
                        for k in range(G):
                            i = te.matmul(P1[0:8, t * TC:(t + 1) * TC], wr32[:, k, :],
                                          XB[:, k, t * TC:(t + 1) * TC], start=(k == 0), stop=(k == G - 1))
                        pe.inc(i)
                    for k in range(G):
                        i = te.matmul(P2[0:1, 0:8], onescol32[:], wr32[:, k, :], start=(k == 0), stop=(k == G - 1))
                    pe.inc(i)
                block.tensor(f_t1)
                pe_raw = pe.v

                def f_v1(v):
                    v.wait_ge(pe.sem, pe_raw)
                    ve.inc(v.tensor_copy(scrA[0:8, :], P1[0:8, :]))
                    ve.inc(v.tensor_copy(wrsrow[:], P2[0:1, 0:8]))
                block.vector(f_v1)
                ve_raw = ve.v

                def f_t2(te):
                    te.wait_ge(ve.sem, ve_raw)
                    for tt in range(NCH):
                        pe.inc(te.transpose(P2[:, tt * 8:(tt + 1) * 8], scrA[0:8, tt * CH:(tt + 1) * CH], id32[0:8, 0:8]))
                    for tt in range(NCH):
                        te.transpose(P2[:, 256 + tt:256 + tt + 1], arow[0:1, tt * CH:(tt + 1) * CH], id32[0:1, 0:1])
                        pe.inc(te.transpose(P2[:, 288 + tt:288 + tt + 1], brow[0:1, tt * CH:(tt + 1) * CH], id32[0:1, 0:1]))
                    pe.inc(te.matmul(P1[:, 1024:1032], ones32[0:1, 0:128], wrsrow[0:1, 0:8], start=True, stop=True))
                block.tensor(f_t2)
                pe_tp = pe.v

                def f_v2(v):
                    v.wait_ge(pe.sem, pe_tp)
                    v.tensor_copy(lgt[:], P2[:, 0:128])
                    v.tensor_copy(abt_a[:], P2[:, 256:272])
                    v.tensor_copy(abt_b[:], P2[:, 288:304])
                    ve.inc(v.tensor_copy(wrsbc[:], P1[:, 1024:1032]))
                    v.wait_ge(ve.sem, ve.v)
                    lg3 = LG.ap().rearrange("p (t e) -> p t e", e=8)
                    raw3 = lgt.ap().rearrange("p (t e) -> p t e", e=8)
                    a3 = abt_a.ap().rearrange("p (t o) -> p t o", o=1)
                    b3 = abt_b.ap().rearrange("p (t o) -> p t o", o=1)
                    for jj in range(8):
                        v.tensor_tensor(lg3[:, :, jj:jj + 1], raw3[:, :, jj:jj + 1], a3, ALU.mult)
                    ve.inc(v.engine_nop())
                    v.wait_ge(ve.sem, ve.v)
                    for jj in range(8):
                        v.scalar_tensor_tensor(lg3[:, :, jj:jj + 1], b3, wrsbc[:, jj:jj + 1],
                                               lg3[:, :, jj:jj + 1], ALU.mult, ALU.add)
                    ve.inc(v.engine_nop())
                    v.wait_ge(ve.sem, ve.v)
                    mx3 = mx16.ap().rearrange("p (t o) -> p t o", o=1)
                    ve.inc(v.reduce_max(mx3, lg3, axis=AX))
                    v.wait_ge(ve.sem, ve.v)
                    sb3 = SBB.ap().rearrange("p (t e) -> p t e", e=8)
                    for jj in range(8):
                        v.tensor_tensor(sb3[:, :, jj:jj + 1], lg3[:, :, jj:jj + 1], mx3, ALU.subtract)
                    ve.inc(v.engine_nop())
                block.vector(f_v2)
                ve_sub = ve.v

                def f_s1(se_):
                    se_.wait_ge(ve.sem, ve_sub)
                    se.inc(se_.activation(et[:], SBB[:], AF.Exp))
                block.scalar(f_s1)
                se_expv = se.v

                def f_v3(v):
                    v.wait_ge(se.sem, se_expv)
                    et3 = et.ap().rearrange("p (t e) -> p t e", e=8)
                    z3 = z16.ap().rearrange("p (t o) -> p t o", o=1)
                    ve.inc(v.reduce_sum(z3, et3, axis=AX))
                    v.wait_ge(ve.sem, ve.v)
                    ve.inc(v.reciprocal(z16[:], z16[:]))
                    v.wait_ge(ve.sem, ve.v)
                    ve.inc(v.tensor_scalar(c16[:], z16[:], 2.0, None, ALU.mult))
                    ve.inc(v.tensor_scalar(mk[:], et[:], 1.0, None, ALU.is_equal))
                    v.wait_ge(ve.sem, ve.v)
                    mk3 = mk.ap().rearrange("p (t e) -> p t e", e=8)
                    pa3 = pa.ap().rearrange("p (t e) -> p t e", e=8)
                    pb3 = pb.ap().rearrange("p (t e) -> p t e", e=8)
                    v.tensor_copy(pa3[:, :, 0:1], mk3[:, :, 0:1])
                    ve.inc(v.tensor_tensor(pa3[:, :, 1:8], mk3[:, :, 1:8], mk3[:, :, 0:7], ALU.add))
                    v.wait_ge(ve.sem, ve.v)
                    v.tensor_copy(pb3[:, :, 0:2], pa3[:, :, 0:2])
                    ve.inc(v.tensor_tensor(pb3[:, :, 2:8], pa3[:, :, 2:8], pa3[:, :, 0:6], ALU.add))
                    v.wait_ge(ve.sem, ve.v)
                    v.tensor_copy(pa3[:, :, 0:4], pb3[:, :, 0:4])
                    ve.inc(v.tensor_tensor(pa3[:, :, 4:8], pb3[:, :, 4:8], pb3[:, :, 0:4], ALU.add))
                    v.wait_ge(ve.sem, ve.v)
                    ve.inc(v.tensor_scalar(pb[:], pa[:], 1.0, None, ALU.is_equal))
                    v.wait_ge(ve.sem, ve.v)
                    ve.inc(v.tensor_tensor(mk[:], mk[:], pb[:], ALU.mult))
                    v.wait_ge(ve.sem, ve.v)
                    ctm3 = ctm.ap().rearrange("p (t e) -> p t e", e=8)
                    c3 = c16.ap().rearrange("p (t o) -> p t o", o=1)
                    for jj in range(8):
                        v.tensor_tensor(ctm3[:, :, jj:jj + 1], mk3[:, :, jj:jj + 1], c3, ALU.mult)
                    ve.inc(v.engine_nop())
                block.vector(f_v3)
                ve_ctm = ve.v

                def f_t3(te):
                    te.wait_ge(ve.sem, ve_ctm)
                    for tt in range(NCH):
                        pe.inc(te.transpose(P1[0:8, tt * CH:(tt + 1) * CH], ctm[:, tt * 8:(tt + 1) * 8], id32[:]))
                block.tensor(f_t3)
                pe_ct = pe.v

                def f_v4(v):
                    v.wait_ge(pe.sem, pe_ct)
                    ve.inc(v.tensor_copy(scrA[0:8, :], P1[0:8, :]))
                block.vector(f_v4)
                ve_crows = ve.v

                for e in range(8):
                    shared = (e == 7)
                    if not shared:
                        pe_now = pe.v

                        def f_sy(sy, e=e, pe_now=pe_now):
                            sy.wait_ge(ve.sem, ve_crows)
                            sy.wait_ge(pe.sem, pe_now)
                            ld.inc(sy.dma_start(out=cbrow[:], in_=scrA[e:e + 1, :]), 16)
                        block.sync(f_sy)
                        ld_cb = ld.v
                        ve_pp2 = ve.v

                        def f_tc(te, ld_cb=ld_cb, ve_pp2=ve_pp2):
                            te.wait_ge(ld.sem, ld_cb)
                            te.wait_ge(ve.sem, ve_pp2)
                            for t in range(NT):
                                i = te.matmul(P1[:, t * TC:(t + 1) * TC], ones32[0:1, 0:128],
                                              cbrow[0:1, t * TC:(t + 1) * TC], start=True, stop=True)
                            pe.inc(i)
                        block.tensor(f_tc)
                        pe_cb = pe.v

                        def f_vc(v, pe_cb=pe_cb):
                            v.wait_ge(pe.sem, pe_cb)
                            ve.inc(v.tensor_copy(scrB[:], P1[:, :]))
                        block.vector(f_vc)
                    ve_cbc = ve.v

                    def f_gw0(g_, e=e, shared=shared):
                        g_.wait_ge(pe.sem, pe.v)
                        if shared:
                            gw.inc(g_.dma_start(out=wub[:], in_=wus_d[:, 0:SLAB].rearrange("(k p) c -> p k c", p=128)), 16)
                            gw.inc(g_.dma_start(out=wgb[:], in_=wgs_d[:, 0:SLAB].rearrange("(k p) c -> p k c", p=128)), 16)
                            gw.inc(g_.dma_start(out=wdb[:], in_=wds_d[0:SLAB, :].rearrange("(k p) c -> p k c", p=128)), 16)
                        else:
                            gw.inc(g_.dma_start(out=wub[:], in_=wu_d[e, :, 0:SLAB].rearrange("(k p) c -> p k c", p=128)), 16)
                            gw.inc(g_.dma_start(out=wgb[:], in_=wg_d[e, :, 0:SLAB].rearrange("(k p) c -> p k c", p=128)), 16)
                            gw.inc(g_.dma_start(out=wdb[:], in_=wd_d[e, 0:SLAB, :].rearrange("(k p) c -> p k c", p=128)), 16)
                    block.gpsimd(f_gw0)

                    for s in range(NSLAB):
                        gw_u, gw_g, gw_d = gw.v - 32, gw.v - 16, gw.v

                        ve_up = {}
                        for m in range(4):
                            def f_tu(te, m=m, s=s, gw_u=gw_u, ve_cbc=ve_cbc):
                                if m == 0:
                                    te.wait_ge(gw.sem, gw_u)
                                    if s == 0:
                                        te.wait_ge(ve.sem, ve_cbc)
                                if m >= 2:
                                    te.wait_ge(ve.sem, ve_up[m - 2])
                                PP = P1 if m % 2 == 0 else P2
                                for t in range(NT):
                                    for k in range(G):
                                        i = te.matmul(PP[:, t * TC:(t + 1) * TC], wub[:, k, m * 128:(m + 1) * 128],
                                                      XN[:, k, t * TC:(t + 1) * TC], start=(k == 0), stop=(k == G - 1))
                                    pe.inc(i)
                            block.tensor(f_tu)
                            pe_um = pe.v

                            def f_vu(v, m=m, pe_um=pe_um, shared=shared):
                                v.wait_ge(pe.sem, pe_um)
                                PP = P1 if m % 2 == 0 else P2
                                if shared:
                                    ve.inc(v.tensor_copy(FM[:, m, :], PP[:, :]))
                                else:
                                    ve.inc(v.tensor_tensor(FM[:, m, :], PP[:, :], scrB[:], ALU.mult))
                                ve_up[m] = ve.v
                            block.vector(f_vu)

                        se_gt = {}
                        for m in range(4):
                            def f_tg(te, m=m, gw_g=gw_g):
                                if m == 0:
                                    te.wait_ge(gw.sem, gw_g)
                                    te.wait_ge(ve.sem, ve_up[3])
                                if m >= 2:
                                    te.wait_ge(se.sem, se_gt[m - 2])
                                PP = P1 if m % 2 == 0 else P2
                                for t in range(NT):
                                    for k in range(G):
                                        i = te.matmul(PP[:, t * TC:(t + 1) * TC], wgb[:, k, m * 128:(m + 1) * 128],
                                                      XN[:, k, t * TC:(t + 1) * TC], start=(k == 0), stop=(k == G - 1))
                                    pe.inc(i)
                            block.tensor(f_tg)
                            pe_gm = pe.v

                            def f_sg(se_, m=m, pe_gm=pe_gm):
                                se_.wait_ge(pe.sem, pe_gm)
                                se.inc(se_.activation(FM[:, 4 + m, :], (P1 if m % 2 == 0 else P2)[:, :], AF.Silu))
                                se_gt[m] = se.v
                            block.scalar(f_sg)

                        def f_vh(v, se_all=None):
                            v.wait_ge(se.sem, se_gt[3])
                            for m in range(4):
                                i = v.tensor_tensor(FM[:, m, :], FM[:, m, :], FM[:, 4 + m, :], ALU.mult)
                            ve.inc(i)
                        block.vector(f_vh)
                        ve_hid = ve.v

                        if DBG_DUMP == "hid0" and e == 0 and s == 0:
                            def f_vdbg2(v):
                                v.wait_ge(ve.sem, ve_hid)
                                ve.inc(v.tensor_copy(scrA[:], FM[:, 0, :]))
                            block.vector(f_vdbg2)

                        pe_gt_done = pe.v

                        def f_gwn(g_, e=e, s=s, shared=shared, pe_gt_done=pe_gt_done):
                            if s + 1 < NSLAB:
                                g_.wait_ge(pe.sem, pe_gt_done)
                                c0 = (s + 1) * SLAB
                                if shared:
                                    gw.inc(g_.dma_start(out=wub[:], in_=wus_d[:, c0:c0 + SLAB].rearrange("(k p) c -> p k c", p=128)), 16)
                                    gw.inc(g_.dma_start(out=wgb[:], in_=wgs_d[:, c0:c0 + SLAB].rearrange("(k p) c -> p k c", p=128)), 16)
                                else:
                                    gw.inc(g_.dma_start(out=wub[:], in_=wu_d[e, :, c0:c0 + SLAB].rearrange("(k p) c -> p k c", p=128)), 16)
                                    gw.inc(g_.dma_start(out=wgb[:], in_=wg_d[e, :, c0:c0 + SLAB].rearrange("(k p) c -> p k c", p=128)), 16)
                        block.gpsimd(f_gwn)

                        ve_dn = {}
                        for mo in range(G):
                            def f_td(te, mo=mo, gw_d=gw_d, ve_hid=ve_hid):
                                if mo == 0:
                                    te.wait_ge(gw.sem, gw_d)
                                    te.wait_ge(ve.sem, ve_hid)
                                if mo >= 2:
                                    te.wait_ge(ve.sem, ve_dn[mo - 2])
                                PP = P1 if mo % 2 == 0 else P2
                                for t in range(NT):
                                    for k in range(4):
                                        i = te.matmul(PP[:, t * TC:(t + 1) * TC], wdb[:, k, mo * 128:(mo + 1) * 128],
                                                      FM[:, k, t * TC:(t + 1) * TC], start=(k == 0), stop=(k == 3))
                                    pe.inc(i)
                            block.tensor(f_td)
                            pe_dm = pe.v

                            def f_vd(v, mo=mo, pe_dm=pe_dm):
                                v.wait_ge(pe.sem, pe_dm)
                                PP = P1 if mo % 2 == 0 else P2
                                ve.inc(v.tensor_tensor(XB[:, mo, :], XB[:, mo, :], PP[:, :], ALU.add))
                                ve_dn[mo] = ve.v
                            block.vector(f_vd)

                        pe_dn_done = pe.v

                        if DBG_DUMP == "cbc" and e == 6 and s == 0:
                            def f_vdbg(v):
                                v.wait_ge(ve.sem, ve.v)
                                ve.inc(v.tensor_copy(scrA[:], scrB[:]))
                            block.vector(f_vdbg)

                        def f_gwd(g_, e=e, s=s, shared=shared, pe_dn_done=pe_dn_done):
                            if s + 1 < NSLAB:
                                g_.wait_ge(pe.sem, pe_dn_done)
                                c0 = (s + 1) * SLAB
                                if shared:
                                    gw.inc(g_.dma_start(out=wdb[:], in_=wds_d[c0:c0 + SLAB, :].rearrange("(k p) c -> p k c", p=128)), 16)
                                else:
                                    gw.inc(g_.dma_start(out=wdb[:], in_=wd_d[e, c0:c0 + SLAB, :].rearrange("(k p) c -> p k c", p=128)), 16)
                        block.gpsimd(f_gwd)

        def emit_store():
            s0 = snap()
            if DBG_DUMP == "abrows":
                def f_sab(sy):
                    wsnap(sy, s0)
                    st.inc(sy.dma_start(out=y_d[0:1, :], in_=arow[:]), 16)
                    st.inc(sy.dma_start(out=y_d[1:2, :], in_=brow[:]), 16)
                    sy.wait_ge(st.sem, st.v)
                block.sync(f_sab)
                return
            if DBG_DUMP in ("scrA", "pkv", "cbc", "hid0"):
                def f_sd(sy):
                    wsnap(sy, s0)
                    st.inc(sy.dma_start(out=y_d[0:128, :], in_=scrA[:]), 16)
                    sy.wait_ge(st.sem, st.v)
                block.sync(f_sd)
                return
            if DBG_DUMP is not None:
                buf = {"FM": FM, "XN": XN}[DBG_DUMP]
                for g in range(G):
                    stv = st.v

                    def f_vd(v, g=g, stv=stv):
                        if g == 0:
                            wsnap(v, s0)
                        v.wait_ge(st.sem, stv)
                        ve.inc(v.tensor_copy(scrA[:], buf[:, g, :]))
                    block.vector(f_vd)
                    vev = ve.v

                    def f_sd(sy, g=g, vev=vev):
                        sy.wait_ge(ve.sem, vev)
                        st.inc(sy.dma_start(out=y_d[g * 128:(g + 1) * 128, :], in_=scrA[:]), 16)
                    block.sync(f_sd)

                def f_fin(sy):
                    sy.wait_ge(st.sem, st.v)
                block.sync(f_fin)
                return
            if DBG_XN:
                def f_vx(v):
                    wsnap(v, s0)
                    for g in range(G):
                        i = v.tensor_copy(scrA[:], XN[:, g, :])
                        ve.inc(i)
                block.vector(f_vx)

            def f_sy(sy):
                wsnap(sy, s0)
                if DBG_XN:
                    sy.wait_ge(ve.sem, ve.v)
                for g in range(G):
                    st.inc(sy.dma_start(out=y_d[g * 128:(g + 1) * 128, :], in_=XB[:, g, :]), 16)
                sy.wait_ge(st.sem, st.v)
            block.sync(f_sy)

        if nsub >= 1:
            emit_ln()
            if DBG_STOP != "ln":
                emit_attn_a()
        if nsub >= 2:
            emit_ln()
            emit_ffn()
        if nsub >= 3:
            emit_ln()
            emit_attn_b()
        if nsub >= 4:
            emit_ln()
            emit_ffn()
        emit_store()

    return nc


def kernel(**inputs):
    cst = host_consts()
    xs = np.ascontiguousarray(np.transpose(np.asarray(inputs["L0_attn_a_x"], np.float32), (0, 2, 1)))
    pkvT1 = np.asarray(inputs["L0_attn_a_past_kv"], np.float32).transpose(1, 0, 2).reshape(64, 1024)
    pkvT = np.ascontiguousarray(np.concatenate([pkvT1, pkvT1], axis=0))
    base = dict(
        wq=np.ascontiguousarray(inputs["L0_attn_a_Wq"], np.float32),
        wk=np.ascontiguousarray(inputs["L0_attn_a_Wk"], np.float32),
        wv=np.ascontiguousarray(inputs["L0_attn_a_Wv"], np.float32),
        pkvT=pkvT,
        gnw=np.ascontiguousarray(np.asarray(inputs["L0_attn_a_gn_w"], np.float32).reshape(H, 1)),
        gnb=np.ascontiguousarray(np.asarray(inputs["L0_attn_a_gn_b"], np.float32).reshape(H, 1)),
        wwT=np.ascontiguousarray(np.asarray(inputs["L0_attn_b_W_w"], np.float32).T),
        wb=np.ascontiguousarray(np.asarray(inputs["L0_attn_b_W_b"], np.float32).reshape(1, 3072)),
        owT=np.ascontiguousarray(np.asarray(inputs["L0_attn_b_out_w"], np.float32).T),
        ob=np.ascontiguousarray(np.asarray(inputs["L0_attn_b_out_b"], np.float32).reshape(1, 1024)),
        wr=np.ascontiguousarray(inputs["L0_ffn_Wr"], np.float32),
        wu=np.ascontiguousarray(inputs["L0_ffn_Wu"], np.float32),
        wg=np.ascontiguousarray(inputs["L0_ffn_Wg"], np.float32),
        wd=np.ascontiguousarray(inputs["L0_ffn_Wd"], np.float32),
        wus=np.ascontiguousarray(inputs["L0_ffn_Wu_s"], np.float32),
        wgs=np.ascontiguousarray(inputs["L0_ffn_Wg_s"], np.float32),
        wds=np.ascontiguousarray(inputs["L0_ffn_Wd_s"], np.float32),
        dmask=cst["dmask"], dmaskT=cst["dmaskT"], gq=cst["gq"], gc=cst["gc"],
        gk=cst["gk"], gcu=cst["gcu"], sel=cst["sel"], ident=cst["ident"], ones32=cst["ones32"],
    )
    nc = build_nc(NSUB)
    in_maps = [dict(base, x=np.ascontiguousarray(xs[b])) for b in range(8)]
    res = run_bass_kernel_spmd(nc, in_maps, core_ids=list(range(8)))
    out = np.stack([np.asarray(r["y"], np.float32).T for r in res.results], axis=0)
    return out


# version 12
# speedup vs baseline: 1.1067x; 1.0841x over previous
import numpy as np
import ml_dtypes
from contextlib import ExitStack

import concourse.bass as bass
import concourse.mybir as mybir
from concourse.bass_utils import run_bass_kernel_spmd

F32 = mybir.dt.float32
BF16 = mybir.dt.bfloat16
AF = mybir.ActivationFunctionType
ALU = mybir.AluOpType
AX = mybir.AxisListType.X

G = 8
T = 2048
TC = 512
NT = 4
CH = 128
NCH = 16
HP = 8
H, DH = 16, 64
DFF = 4096
SLAB = 512
NSLAB = DFF // SLAB
EPS = 1e-5
NTOK_GN = float(H * DH * T)

NSUB = 4
DBG_STOP = None
DBG_XN = False
DBG_DUMP = None


class Ctr:
    def __init__(self, sem):
        self.sem = sem
        self.v = 0

    def inc(self, inst, n=1):
        inst.then_inc(self.sem, n)
        self.v += n
        return self.v


def host_consts():
    gamma = (1.0 - 2.0 ** (-5.0 - np.arange(H, dtype=np.float32))).astype(np.float32)
    j = np.arange(CH, dtype=np.float32)
    diff = j[:, None] - j[None, :]
    dmask = np.where(diff[None] >= 0, gamma[:, None, None] ** np.maximum(diff[None], 0.0), 0.0).astype(np.float32)
    dmaskT = np.ascontiguousarray(np.transpose(dmask, (0, 2, 1)))
    gq = np.zeros((HP, 128, CH), np.float32)
    gc = np.zeros((HP, 128, CH), np.float32)
    gk = np.zeros((HP, 128, CH), np.float32)
    gcu = np.zeros((128, HP), np.float32)
    for hp in range(HP):
        for half in range(2):
            g_ = gamma[2 * hp + half]
            rows = slice(64 * half, 64 * half + 64)
            gq[hp, rows, :] = g_ ** j[None, :]
            gc[hp, rows, :] = g_ ** (CH - j[None, :])
            gk[hp, :, rows] = (g_ ** (CH - j[:, None]))
            gcu[rows, hp] = g_ ** CH
    sel = np.zeros((G, H, 128), np.float32)
    for g in range(G):
        for p in range(128):
            sel[g, (128 * g + p) // DH, p] = 1.0
    ident = np.eye(128, dtype=np.float32)
    ones32 = np.ones((1, TC), np.float32)
    return dict(dmask=dmask, dmaskT=dmaskT, gq=gq, gc=gc, gk=gk, gcu=gcu,
                sel=sel, ident=ident, ones32=ones32)


def build_nc(nsub=4):
    nc = bass.Bass("TRN2", target_bir_lowering=False)

    def din(name, shape):
        return nc.dram_tensor(name, list(shape), F32, kind="ExternalInput").ap()

    x_d = din("x", [1024, T])
    wq_d = din("wq", [1024, 1024])
    wk_d = din("wk", [1024, 1024])
    wv_d = din("wv", [1024, 1024])
    pkv_d = din("pkvT", [128, 1024])
    gnw_d = din("gnw", [H, 1])
    gnb_d = din("gnb", [H, 1])
    wwT_d = din("wwT", [1024, 3072])
    wb_d = din("wb", [1, 3072])
    owT_d = din("owT", [1024, 1024])
    ob_d = din("ob", [1, 1024])
    wr_d = din("wr", [1024, 8])
    wu_d = din("wu", [7, 1024, DFF])
    wg_d = din("wg", [7, 1024, DFF])
    wd_d = din("wd", [7, DFF, 1024])
    wus_d = din("wus", [1024, DFF])
    wgs_d = din("wgs", [1024, DFF])
    wds_d = din("wds", [DFF, 1024])
    dmask_d = din("dmask", [H, CH, CH])
    dmaskT_d = din("dmaskT", [H, CH, CH])
    gq_d = din("gq", [HP, 128, CH])
    gc_d = din("gc", [HP, 128, CH])
    gk_d = din("gk", [HP, 128, CH])
    gcu_d = din("gcu", [128, HP])
    sel_d = din("sel", [G, H, 128])
    ident_d = din("ident", [128, 128])
    ones32_d = din("ones32", [1, TC])
    y_d = nc.dram_tensor("y", [1024, T], F32, kind="ExternalOutput").ap()

    with ExitStack() as ctx:
        sb = lambda n, s, d: ctx.enter_context(nc.sbuf_tensor("sb_" + n, s, d))
        psm = lambda n, s, d: ctx.enter_context(nc.psum_tensor(n, s, d))
        sem = lambda n: ctx.enter_context(nc.semaphore(n))

        XB = sb("XB", [128, G, T], F32)
        XN = sb("XN", [128, G, T], BF16)
        FM = sb("FM", [128, G, T], BF16)
        id32 = sb("id32", [128, 128], F32)
        ones32 = sb("ones32", [1, TC], F32)
        onesbf = sb("onesbf", [1, TC], BF16)
        onescol = sb("onescol", [128, 1], BF16)
        onescol32 = sb("onescol32", [128, 1], F32)
        arow = sb("arow", [1, T], F32)
        brow = sb("brow", [1, T], F32)
        scrA = sb("scrA", [128, T], F32)
        vals = sb("vals", [1, 16], F32)
        gnw16 = sb("gnw16", [H, 1], F32)
        gnb16 = sb("gnb16", [H, 1], F32)
        rb16 = sb("rb16", [H, 2], F32)
        stat2 = sb("stat2", [H, 2], F32)
        stcol = sb("stcol", [128, 2 * G], F32)
        wr32 = sb("wr32", [128, 8, 8], F32)
        wrsrow = sb("wrsrow", [1, 8], F32)
        wrsbc = sb("wrsbc", [128, 8], F32)
        epsc = sb("epsc", [1, 1], F32)

        P1 = psm("P1", [128, T], F32)
        P2 = psm("P2", [128, T], F32)

        ld = Ctr(sem("s_ld"))
        gw = Ctr(sem("s_gw"))
        pe = Ctr(sem("s_pe"))
        ve = Ctr(sem("s_ve"))
        se = Ctr(sem("s_se"))
        gp = Ctr(sem("s_gp"))
        st = Ctr(sem("s_st"))
        ALLC = dict(ld=ld, gw=gw, pe=pe, ve=ve, se=se, gp=gp, st=st)

        block = ctx.enter_context(nc.Block())

        phase_id = [0]

        def snap():
            return {k: c.v for k, c in ALLC.items()}

        def wsnap(eng, s):
            for k, v in s.items():
                if v > 0:
                    eng.wait_ge(ALLC[k].sem, v)

        def f_sync0(sy):
            ld.inc(sy.dma_start(out=XB[:], in_=x_d.rearrange("(g p) t -> p g t", p=128)), 16)
            ld.inc(sy.dma_start(out=id32[:], in_=ident_d), 16)
            ld.inc(sy.dma_start(out=ones32[:], in_=ones32_d), 16)
            ld.inc(sy.dma_start(out=gnw16[:], in_=gnw_d), 16)
            ld.inc(sy.dma_start(out=gnb16[:], in_=gnb_d), 16)
            ld.inc(sy.dma_start(out=wr32[:], in_=wr_d.rearrange("(k p) e -> p k e", p=128)), 16)
        block.sync(f_sync0)

        def f_g0(g_):
            gw.inc(g_.dma_start(out=onesbf[:], in_=ones32_d), 16)
            gp.inc(g_.memset(onescol[:], 1.0), 1)
            gp.inc(g_.memset(onescol32[:], 1.0), 1)
            gp.inc(g_.memset(epsc[:], EPS), 1)
        block.gpsimd(f_g0)

        def emit_ln():
            s0 = snap()
            phase_id[0] += 1
            pid = str(phase_id[0])
            with nc.sbuf_tensor("ln_sum" + pid, [1, T], F32) as sumrow, \
                 nc.sbuf_tensor("ln_sq" + pid, [1, T], F32) as sqrow, \
                 nc.sbuf_tensor("ln_tmp" + pid, [1, T], F32) as tmprow:

                def f_s(se_):
                    wsnap(se_, s0)
                    for g in range(G):
                        i = se_.activation(XN[:, g, :], XB[:, g, :], AF.Copy)
                    for g in range(G):
                        i = se_.activation(FM[:, g, :], XB[:, g, :], AF.Square)
                    se.inc(i)
                block.scalar(f_s)
                se_ready = se.v

                def f_t(te):
                    wsnap(te, s0)
                    te.wait_ge(se.sem, se_ready)
                    for t in range(NT):
                        for g in range(G):
                            i = te.matmul(P1[0:1, t * TC:(t + 1) * TC], onescol[:],
                                          XN[:, g, t * TC:(t + 1) * TC], start=(g == 0), stop=(g == G - 1))
                        pe.inc(i)
                        for g in range(G):
                            i = te.matmul(P2[0:1, t * TC:(t + 1) * TC], onescol[:],
                                          FM[:, g, t * TC:(t + 1) * TC], start=(g == 0), stop=(g == G - 1))
                        pe.inc(i)
                block.tensor(f_t)
                pe_stats = pe.v

                def f_v(v):
                    v.wait_ge(pe.sem, pe_stats)
                    ve.inc(v.tensor_scalar(sumrow[:], P1[0:1, :], 1.0 / 1024.0, None, ALU.mult))
                    ve.inc(v.tensor_scalar(sqrow[:], P2[0:1, :], 1.0 / 1024.0, None, ALU.mult))
                    v.wait_ge(ve.sem, ve.v)
                    ve.inc(v.tensor_tensor(tmprow[:], sumrow[:], sumrow[:], ALU.mult))
                    v.wait_ge(ve.sem, ve.v)
                    ve.inc(v.tensor_tensor(sqrow[:], sqrow[:], tmprow[:], ALU.subtract))
                block.vector(f_v)
                ve_var = ve.v

                def f_s2(se_):
                    se_.wait_ge(ve.sem, ve_var)
                    se.inc(se_.activation(tmprow[:], sqrow[:], AF.Sqrt, bias=epsc[0:1, 0:1]))
                block.scalar(f_s2)
                se_sqrt = se.v

                def f_v2(v):
                    v.wait_ge(se.sem, se_sqrt)
                    ve.inc(v.reciprocal(arow[:], tmprow[:]))
                    v.wait_ge(ve.sem, ve.v)
                    ve.inc(v.scalar_tensor_tensor(brow[:], sumrow[:], -1.0, arow[:], ALU.mult, ALU.mult))
                block.vector(f_v2)
                ve_ab = ve.v

                def f_t2(te):
                    te.wait_ge(ve.sem, ve_ab)
                    for t in range(NT):
                        pe.inc(te.matmul(P1[:, t * TC:(t + 1) * TC], ones32[0:1, 0:128],
                                         arow[0:1, t * TC:(t + 1) * TC], start=True, stop=True))
                        pe.inc(te.matmul(P2[:, t * TC:(t + 1) * TC], ones32[0:1, 0:128],
                                         brow[0:1, t * TC:(t + 1) * TC], start=True, stop=True))
                block.tensor(f_t2)
                pe_bc = pe.v

                def f_v3(v):
                    v.wait_ge(pe.sem, pe_bc)
                    for g in range(G):
                        ve.inc(v.tensor_tensor(XN[:, g, :], XN[:, g, :], P1[:, :], ALU.mult))
                        v.wait_ge(ve.sem, ve.v)
                        ve.inc(v.tensor_tensor(XN[:, g, :], XN[:, g, :], P2[:, :], ALU.add))
                    i = None
                block.vector(f_v3)

        def emit_attn_a():
            s0 = snap()
            with ExitStack() as actx:
                phase_id[0] += 1
                asb = lambda n, s, d: actx.enter_context(nc.sbuf_tensor(n + str(phase_id[0]), s, d))
                dmT = asb("dmT", [128, H, CH], BF16)
                dmv = asb("dmv", [128, H, CH], BF16)
                gqs = asb("gqs", [128, HP, CH], BF16)
                gcs = asb("gcs", [128, HP, CH], BF16)
                gks = asb("gks", [128, HP, CH], BF16)
                gcus = asb("gcus", [128, HP], F32)
                sel32 = asb("sel32", [H, G, 128], F32)
                pkv = asb("pkv", [128, 1024], BF16)
                wqb = asb("wqb", [128, 8, 128], BF16)
                wkb = asb("wkb", [128, 8, 128], BF16)
                wvb = asb("wvb", [128, 8, 128], BF16)
                qf = asb("qf", [128, T], BF16)
                kf = asb("kf", [128, T], BF16)
                kt = asb("kt", [128, T], BF16)
                vt = asb("vt", [128, T], BF16)
                ppt = asb("ppt", [128, T], BF16)
                qt1 = asb("qt1", [128, CH], BF16)
                ab1 = asb("ab1", [128, CH], BF16)
                ab2_ = asb("ab2_", [128, CH], BF16)
                tmc = asb("tmc", [128, CH], F32)
                carry = asb("carry", [128, 1], F32)
                N32 = asb("N32", [128, DH], F32)
                Nbf = asb("Nbf", [128, DH], BF16)

                def f_g(g_):
                    wsnap(g_, s0)
                    gw.inc(g_.dma_start(out=dmT[:], in_=dmaskT_d.rearrange("h s t -> s h t")), 16)
                    gw.inc(g_.dma_start(out=dmv[:], in_=dmask_d.rearrange("h s t -> s h t")), 16)
                    gw.inc(g_.dma_start(out=gqs[:], in_=gq_d.rearrange("h p c -> p h c")), 16)
                    gw.inc(g_.dma_start(out=gcs[:], in_=gc_d.rearrange("h p c -> p h c")), 16)
                    gw.inc(g_.dma_start(out=gks[:], in_=gk_d.rearrange("h p c -> p h c")), 16)
                    gw.inc(g_.dma_start(out=pkv[:], in_=pkv_d), 16)
                    gw.inc(g_.dma_start(out=wqb[:], in_=wq_d[:, 0:128].rearrange("(a p) c -> p a c", p=128)), 16)
                    gw.inc(g_.dma_start(out=wkb[:], in_=wk_d[:, 0:128].rearrange("(a p) c -> p a c", p=128)), 16)
                    gw.inc(g_.dma_start(out=wvb[:], in_=wv_d[:, 0:128].rearrange("(a p) c -> p a c", p=128)), 16)
                block.gpsimd(f_g)
                gw_consts = gw.v

                def f_sy(sy):
                    wsnap(sy, s0)
                    ld.inc(sy.dma_start(out=gcus[:], in_=gcu_d), 16)
                    ld.inc(sy.dma_start(out=sel32[:], in_=sel_d.rearrange("g h c -> h g c")), 16)
                block.sync(f_sy)
                ld_consts = ld.v

                gw_w = gw_consts
                for hp in range(HP):
                    hA, hB = 2 * hp, 2 * hp + 1
                    ve_prev = ve.v

                    def f_t1(te, hp=hp, gw_w=gw_w, ve_prev=ve_prev):
                        te.wait_ge(gw.sem, gw_w)
                        te.wait_ge(ve.sem, ve_prev)
                        if hp == 0:
                            wsnap(te, s0)
                            te.wait_ge(ld.sem, ld_consts)
                        for t in range(NT):
                            for k in range(G):
                                i = te.matmul(P1[:, t * TC:(t + 1) * TC], wqb[:, k, :],
                                              XN[:, k, t * TC:(t + 1) * TC], start=(k == 0), stop=(k == G - 1))
                            pe.inc(i)
                            for k in range(G):
                                i = te.matmul(P2[:, t * TC:(t + 1) * TC], wkb[:, k, :],
                                              XN[:, k, t * TC:(t + 1) * TC], start=(k == 0), stop=(k == G - 1))
                            pe.inc(i)
                    block.tensor(f_t1)
                    pe_qk = pe.v

                    def f_v1(v, pe_qk=pe_qk):
                        v.wait_ge(pe.sem, pe_qk)
                        ve.inc(v.tensor_copy(qf[:], P1[:, :]))
                        ve.inc(v.tensor_copy(kf[:], P2[:, :]))
                    block.vector(f_v1)
                    ve_qk = ve.v

                    def f_t2(te, ve_qk=ve_qk):
                        te.wait_ge(ve.sem, ve_qk)
                        for tt in range(NCH):
                            for k in range(G):
                                i = te.matmul(P1[:, tt * CH:(tt + 1) * CH], XN[:, k, tt * CH:(tt + 1) * CH],
                                              wkb[:, k, :], start=(k == 0), stop=(k == G - 1))
                            pe.inc(i)
                            for k in range(G):
                                i = te.matmul(P2[:, tt * CH:(tt + 1) * CH], XN[:, k, tt * CH:(tt + 1) * CH],
                                              wvb[:, k, :], start=(k == 0), stop=(k == G - 1))
                            pe.inc(i)
                    block.tensor(f_t2)
                    pe_kv = pe.v

                    def f_v2(v, pe_kv=pe_kv):
                        v.wait_ge(pe.sem, pe_kv)
                        ve.inc(v.tensor_copy(kt[:], P1[:, :]))
                        ve.inc(v.tensor_copy(vt[:], P2[:, :]))
                    block.vector(f_v2)
                    ve_kv = ve.v

                    if hp + 1 < HP:
                        def f_gp(g_, hp=hp, pe_kv=pe_kv):
                            g_.wait_ge(pe.sem, pe_kv)
                            c0 = 128 * (hp + 1)
                            gw.inc(g_.dma_start(out=wqb[:], in_=wq_d[:, c0:c0 + 128].rearrange("(a p) c -> p a c", p=128)), 16)
                            gw.inc(g_.dma_start(out=wkb[:], in_=wk_d[:, c0:c0 + 128].rearrange("(a p) c -> p a c", p=128)), 16)
                            gw.inc(g_.dma_start(out=wvb[:], in_=wv_d[:, c0:c0 + 128].rearrange("(a p) c -> p a c", p=128)), 16)
                        block.gpsimd(f_gp)
                        gw_w = gw.v

                    if DBG_STOP == "aa_proj":
                        continue

                    def f_t3(te, ve_kv=ve_kv, hA=hA, hB=hB):
                        te.wait_ge(ve.sem, ve_kv)
                        for I in range(NCH):
                            te.matmul(P1[:, I * 64:(I + 1) * 64], qf[0:64, I * CH:(I + 1) * CH],
                                      pkv[0:64, hA * 64:hA * 64 + 64], start=True, stop=True)
                            i = te.matmul(P1[:, 1024 + I * 64:1024 + (I + 1) * 64], qf[64:128, I * CH:(I + 1) * CH],
                                          pkv[64:128, hB * 64:hB * 64 + 64], start=True, stop=True)
                        pe.inc(i)
                    block.tensor(f_t3)
                    pe_pp = pe.v

                    def f_v3(v, pe_pp=pe_pp, hp=hp):
                        v.wait_ge(pe.sem, pe_pp)
                        pp4 = ppt.ap().rearrange("p (c h d) -> p c h d", h=2, d=64)
                        v.tensor_copy(pp4[:, :, 0, :], P1.ap().rearrange("p (x c d) -> p x c d", x=2, d=64)[:, 0, :, :])
                        ve.inc(v.tensor_copy(pp4[:, :, 1, :], P1.ap().rearrange("p (x c d) -> p x c d", x=2, d=64)[:, 1, :, :]))
                        ve.inc(v.memset(carry[:], 0.0))
                        if DBG_DUMP == "scrA" and DBG_STOP == "aa_pp" and hp == HP - 1:
                            v.wait_ge(ve.sem, ve.v)
                            ve.inc(v.tensor_copy(scrA[:], ppt[:]))
                        if DBG_DUMP == "pkv" and hp == HP - 1:
                            v.wait_ge(ve.sem, ve.v)
                            ve.inc(v.tensor_copy(scrA[:, 0:1024], pkv[:]))
                            ve.inc(v.tensor_copy(scrA[:, 1024:2048], qf[:, 0:1024]))
                    block.vector(f_v3)
                    ve_pp = ve.v

                    if DBG_STOP == "aa_pp":
                        continue

                    def f_t4(te, ve_pp=ve_pp, hA=hA, hB=hB):
                        te.wait_ge(ve.sem, ve_pp)
                        for I in range(NCH - 1, -1, -1):
                            te.matmul(P2[0:64, I * CH:(I + 1) * CH], ppt[:, I * CH:I * CH + 64],
                                      dmv[:, hA, :], start=True, stop=True)
                            pe.inc(te.matmul(P2[64:128, I * CH:(I + 1) * CH], ppt[:, I * CH + 64:(I + 1) * CH],
                                             dmv[:, hB, :], start=True, stop=True))
                    block.tensor(f_t4)
                    pe_scan0 = pe.v - NCH

                    if DBG_STOP == "aa_scan_t":
                        def f_v4x(v, pe_scan0=pe_scan0):
                            v.wait_ge(pe.sem, pe_scan0 + NCH)
                            ve.inc(v.tensor_copy(scrA[:], P2[:, :]))
                        block.vector(f_v4x)
                        continue

                    def f_v4(v, pe_scan0=pe_scan0, hp=hp):
                        v.wait_ge(pe.sem, pe_scan0 + NCH)
                        for idx, I in enumerate(range(NCH - 1, -1, -1)):
                            v.wait_ge(ve.sem, ve.v)
                            ve.inc(v.tensor_scalar(tmc[:], gcs[:, hp, :], carry[:, 0:1], None, ALU.mult))
                            v.wait_ge(ve.sem, ve.v)
                            ve.inc(v.tensor_tensor(scrA[:, I * CH:(I + 1) * CH], P2[:, I * CH:(I + 1) * CH], tmc[:], ALU.add))
                            v.wait_ge(ve.sem, ve.v)
                            ve.inc(v.tensor_copy(carry[:], scrA[:, I * CH:I * CH + 1]))
                    block.vector(f_v4)

                    pe_allscan = pe.v
                    if DBG_STOP == "aa_scan_nokhat":
                        continue

                    def f_v5(v, pe_allscan=pe_allscan, hp=hp):
                        v.wait_ge(pe.sem, pe_allscan)
                        for I in range(NCH):
                            i = v.tensor_tensor(ppt[:, I * CH:(I + 1) * CH], kt[:, I * CH:(I + 1) * CH],
                                                gks[:, hp, :], ALU.mult)
                        ve.inc(i)
                        v.memset(N32[:], 0.0)
                        ve.inc(v.memset(Nbf[:], 0.0))
                    block.vector(f_v5)
                    ve_khat = ve.v

                    if DBG_STOP == "aa_scan":
                        continue
                    pe_sc = {}
                    ve_ab_ = {}
                    ve_ret = {}
                    ve_nup = {}
                    for I in range(NCH):
                        def f_te(te, I=I, hp=hp):
                            if I == 0:
                                te.wait_ge(ve.sem, ve_khat)
                            else:
                                te.wait_ge(ve.sem, ve_ab_[I - 1])
                            te.matmul(P1[:, 0:128], kf[0:64, I * CH:(I + 1) * CH],
                                      qf[0:64, I * CH:(I + 1) * CH], start=True, stop=True)
                            pe.inc(te.matmul(P2[:, 0:128], kf[64:128, I * CH:(I + 1) * CH],
                                             qf[64:128, I * CH:(I + 1) * CH], start=True, stop=True))
                            pe_sc[I] = pe.v
                        block.tensor(f_te)

                        def f_ve(v, I=I, hp=hp):
                            v.wait_ge(pe.sem, pe_sc[I])
                            v.tensor_tensor(ab1[:], P1[:, 0:128], dmT[:, 2 * hp, :], ALU.mult)
                            v.tensor_tensor(ab2_[:], P2[:, 0:128], dmT[:, 2 * hp + 1, :], ALU.mult)
                            ve.inc(v.tensor_tensor(qt1[:], qf[:, I * CH:(I + 1) * CH], gqs[:, hp, :], ALU.mult))
                            ve_ab_[I] = ve.v
                        block.vector(f_ve)

                        def f_te2(te, I=I):
                            te.wait_ge(ve.sem, ve_ab_[I])
                            if I >= 1:
                                te.wait_ge(ve.sem, ve_ret[I - 1])
                            te.matmul(P1[0:64, 512:640], vt[:, I * CH:I * CH + 64], ab1[:], start=True, stop=False)
                            te.matmul(P1[0:64, 512:640], Nbf[0:64, :], qt1[0:64, :], start=False, stop=True)
                            te.matmul(P1[64:128, 512:640], vt[:, I * CH + 64:(I + 1) * CH], ab2_[:], start=True, stop=False)
                            pe.inc(te.matmul(P1[64:128, 512:640], Nbf[64:128, :], qt1[64:128, :], start=False, stop=True))
                            pe_sc[(I, 'r')] = pe.v
                            nb = 1024 if I % 2 == 0 else 1536
                            if I >= 2:
                                te.wait_ge(ve.sem, ve_nup[I - 2])
                            te.matmul(P1[0:64, nb:nb + 64], ppt[:, I * CH:I * CH + 64],
                                      vt[:, I * CH:I * CH + 64], start=True, stop=True)
                            pe.inc(te.matmul(P1[64:128, nb:nb + 64], ppt[:, I * CH + 64:(I + 1) * CH],
                                             vt[:, I * CH + 64:(I + 1) * CH], start=True, stop=True))
                            pe_sc[(I, 'n')] = pe.v
                        block.tensor(f_te2)

                        def f_ve2(v, I=I, hp=hp):
                            v.wait_ge(pe.sem, pe_sc[(I, 'r')])
                            ve.inc(v.tensor_tensor(FM[:, hp, I * CH:(I + 1) * CH], P1[:, 512:640],
                                                   scrA[:, I * CH:(I + 1) * CH], ALU.add))
                            ve_ret[I] = ve.v
                            nb = 1024 if I % 2 == 0 else 1536
                            v.wait_ge(pe.sem, pe_sc[(I, 'n')])
                            v.scalar_tensor_tensor(N32[:], N32[:], gcus[:, hp:hp + 1], P1[:, nb:nb + 64], ALU.mult, ALU.add)
                            v.wait_ge(ve.sem, ve.v)
                            ve.inc(v.tensor_copy(Nbf[:], N32[:]))
                            ve_nup[I] = ve.v
                        block.vector(f_ve2)

                if DBG_STOP == "aa_fwd":
                    return
                sA = snap()

                def f_s3(se_):
                    wsnap(se_, sA)
                    for g in range(G):
                        i = se_.activation(XN[:, g, :], FM[:, g, :], AF.Square)
                    se.inc(i)
                block.scalar(f_s3)
                se_sq = se.v

                def f_t5(te):
                    te.wait_ge(se.sem, se_sq)
                    first = True
                    for g in range(G):
                        for t in range(NT):
                            i = te.matmul(P1[0:1, 0:TC], onescol[:], FM[:, g, t * TC:(t + 1) * TC],
                                          start=first, stop=(g == G - 1 and t == NT - 1))
                            first = False
                    pe.inc(i)
                    first = True
                    for g in range(G):
                        for t in range(NT):
                            i = te.matmul(P2[0:1, 0:TC], onescol[:], XN[:, g, t * TC:(t + 1) * TC],
                                          start=first, stop=(g == G - 1 and t == NT - 1))
                            first = False
                    pe.inc(i)
                block.tensor(f_t5)
                pe_gn = pe.v

                def f_v6(v):
                    v.wait_ge(pe.sem, pe_gn)
                    ve.inc(v.reduce_sum(vals[0:1, 0:1], P1[0:1, 0:TC], axis=AX))
                    ve.inc(v.reduce_sum(vals[0:1, 1:2], P2[0:1, 0:TC], axis=AX))
                    v.wait_ge(ve.sem, ve.v)
                    v.tensor_scalar(vals[0:1, 2:3], vals[0:1, 0:1], 1.0 / NTOK_GN, None, ALU.mult)
                    ve.inc(v.tensor_scalar(vals[0:1, 3:4], vals[0:1, 1:2], 1.0 / NTOK_GN, None, ALU.mult))
                    v.wait_ge(ve.sem, ve.v)
                    ve.inc(v.tensor_tensor(vals[0:1, 4:5], vals[0:1, 2:3], vals[0:1, 2:3], ALU.mult))
                    v.wait_ge(ve.sem, ve.v)
                    ve.inc(v.tensor_tensor(vals[0:1, 5:6], vals[0:1, 3:4], vals[0:1, 4:5], ALU.subtract))
                block.vector(f_v6)
                ve_gnv = ve.v

                def f_s4(se_):
                    se_.wait_ge(ve.sem, ve_gnv)
                    se.inc(se_.activation(vals[0:1, 6:7], vals[0:1, 5:6], AF.Sqrt, bias=epsc[0:1, 0:1]))
                block.scalar(f_s4)
                se_gns = se.v

                def f_v7(v):
                    v.wait_ge(se.sem, se_gns)
                    ve.inc(v.reciprocal(vals[0:1, 8:9], vals[0:1, 6:7]))
                    v.wait_ge(ve.sem, ve.v)
                    ve.inc(v.tensor_copy(vals[0:1, 9:10], vals[0:1, 2:3]))
                block.vector(f_v7)
                ve_gnr = ve.v

                def f_t6(te):
                    te.wait_ge(ve.sem, ve_gnr)
                    pe.inc(te.matmul(P1[0:16, 0:2], ones32[0:1, 0:16], vals[0:1, 8:10], start=True, stop=True))
                block.tensor(f_t6)
                pe_b16 = pe.v

                def f_v8(v):
                    v.wait_ge(pe.sem, pe_b16)
                    ve.inc(v.tensor_copy(rb16[:], P1[0:16, 0:2]))
                    v.wait_ge(ve.sem, ve.v)
                    ve.inc(v.tensor_tensor(stat2[:, 0:1], gnw16[:], rb16[:, 0:1], ALU.mult))
                    v.wait_ge(ve.sem, ve.v)
                    ve.inc(v.tensor_tensor(stat2[:, 1:2], stat2[:, 0:1], rb16[:, 1:2], ALU.mult))
                    v.wait_ge(ve.sem, ve.v)
                    ve.inc(v.tensor_tensor(stat2[:, 1:2], gnb16[:], stat2[:, 1:2], ALU.subtract))
                block.vector(f_v8)
                ve_st2 = ve.v

                def f_t7(te):
                    te.wait_ge(ve.sem, ve_st2)
                    for g in range(G):
                        i = te.matmul(P2[:, 2 * g:2 * g + 2], sel32[:, g, :], stat2[:], start=True, stop=True)
                    pe.inc(i)
                block.tensor(f_t7)
                pe_sel = pe.v

                def f_v9(v):
                    v.wait_ge(pe.sem, pe_sel)
                    ve.inc(v.tensor_copy(stcol[:], P2[:, 0:2 * G]))
                    v.wait_ge(ve.sem, ve.v)
                    for g in range(G):
                        ve.inc(v.scalar_tensor_tensor(XB[:, g, :], FM[:, g, :], stcol[:, 2 * g:2 * g + 1],
                                                      XB[:, g, :], ALU.mult, ALU.add))
                        v.wait_ge(ve.sem, ve.v)
                        ve.inc(v.tensor_scalar(XB[:, g, :], XB[:, g, :], stcol[:, 2 * g + 1:2 * g + 2], None, ALU.add))
                block.vector(f_v9)

        def emit_attn_b():
            s0 = snap()
            with ExitStack() as actx:
                phase_id[0] += 1
                asb = lambda n, s, d: actx.enter_context(nc.sbuf_tensor(n + str(phase_id[0]), s, d))
                wqb = asb("bwq", [128, 8, 128], BF16)
                wkb = asb("bwk", [128, 8, 128], BF16)
                wvb = asb("bwv", [128, 8, 128], BF16)
                owT = asb("owT", [128, 8, 1024], BF16)
                wbq = asb("wbq", [1, 3072], BF16)
                obq = asb("obq", [1, 1024], BF16)
                qf = asb("bqf", [128, T], BF16)
                kf = asb("bkf", [128, T], BF16)
                vt = asb("bvt", [128, NCH, 130], BF16)
                eb = asb("eb", [128, T], BF16)
                zrow = asb("zrow", [1, TC], F32)
                zb = asb("zb", [64, TC], F32)

                def f_g(g_):
                    wsnap(g_, s0)
                    gw.inc(g_.dma_start(out=wbq[:], in_=wb_d), 16)
                    gw.inc(g_.dma_start(out=obq[:], in_=ob_d), 16)
                    gw.inc(g_.dma_start(out=wqb[:], in_=wwT_d[:, 0:128].rearrange("(a p) c -> p a c", p=128)), 16)
                    gw.inc(g_.dma_start(out=wkb[:], in_=wwT_d[:, 1024:1152].rearrange("(a p) c -> p a c", p=128)), 16)
                    gw.inc(g_.dma_start(out=wvb[:], in_=wwT_d[:, 2048:2176].rearrange("(a p) c -> p a c", p=128)), 16)
                    gw.inc(g_.dma_start(out=owT[:], in_=owT_d.rearrange("(a p) c -> p a c", p=128)), 16)
                    gp.inc(g_.memset(vt.ap().rearrange("p t (h c) -> p t h c", c=65)[:, :, :, 64:65], 1.0), 1)
                block.gpsimd(f_g)
                gw_w = gw.v
                gp_ones = gp.v

                for hp in range(HP):
                    ve_prev = ve.v

                    def f_t1(te, hp=hp, gw_w=gw_w, ve_prev=ve_prev):
                        te.wait_ge(gw.sem, gw_w)
                        te.wait_ge(ve.sem, ve_prev)
                        if hp == 0:
                            wsnap(te, s0)
                        for t in range(NT):
                            for k in range(G):
                                te.matmul(P1[:, t * TC:(t + 1) * TC], wqb[:, k, :],
                                          XN[:, k, t * TC:(t + 1) * TC], start=(k == 0), stop=False)
                            pe.inc(te.matmul(P1[:, t * TC:(t + 1) * TC], wbq[0:1, 128 * hp:128 * hp + 128],
                                             onesbf[0:1, 0:TC], start=False, stop=True))
                            for k in range(G):
                                te.matmul(P2[:, t * TC:(t + 1) * TC], wkb[:, k, :],
                                          XN[:, k, t * TC:(t + 1) * TC], start=(k == 0), stop=False)
                            pe.inc(te.matmul(P2[:, t * TC:(t + 1) * TC], wbq[0:1, 1024 + 128 * hp:1024 + 128 * hp + 128],
                                             onesbf[0:1, 0:TC], start=False, stop=True))
                    block.tensor(f_t1)
                    pe_qk = pe.v

                    def f_v1(v, pe_qk=pe_qk):
                        v.wait_ge(pe.sem, pe_qk)
                        ve.inc(v.tensor_copy(qf[:], P1[:, :]))
                        ve.inc(v.tensor_copy(kf[:], P2[:, :]))
                    block.vector(f_v1)
                    ve_qk = ve.v

                    def f_t2(te, hp=hp, ve_qk=ve_qk):
                        te.wait_ge(ve.sem, ve_qk)
                        if hp == 0:
                            te.wait_ge(gp.sem, gp_ones)
                        for tt in range(NCH):
                            for k in range(G):
                                te.matmul(P1[:, tt * CH:(tt + 1) * CH], XN[:, k, tt * CH:(tt + 1) * CH],
                                          wvb[:, k, :], start=(k == 0), stop=False)
                            i = te.matmul(P1[:, tt * CH:(tt + 1) * CH], onesbf[0:1, 0:128],
                                          wbq[0:1, 2048 + 128 * hp:2048 + 128 * hp + 128], start=False, stop=True)
                            pe.inc(i)
                    block.tensor(f_t2)
                    pe_v = pe.v

                    def f_v2(v, pe_v=pe_v):
                        v.wait_ge(pe.sem, pe_v)
                        p3 = P1.ap().rearrange("p (t c) -> p t c", c=CH)
                        v.tensor_copy(vt[:, :, 0:64], p3[:, :, 0:64])
                        ve.inc(v.tensor_copy(vt[:, :, 65:129], p3[:, :, 64:128]))
                    block.vector(f_v2)
                    ve_v = ve.v

                    if hp + 1 < HP:
                        def f_gp(g_, hp=hp, pe_v=pe_v):
                            g_.wait_ge(pe.sem, pe_v)
                            c0 = 128 * (hp + 1)
                            gw.inc(g_.dma_start(out=wqb[:], in_=wwT_d[:, c0:c0 + 128].rearrange("(a p) c -> p a c", p=128)), 16)
                            gw.inc(g_.dma_start(out=wkb[:], in_=wwT_d[:, 1024 + c0:1024 + c0 + 128].rearrange("(a p) c -> p a c", p=128)), 16)
                            gw.inc(g_.dma_start(out=wvb[:], in_=wwT_d[:, 2048 + c0:2048 + c0 + 128].rearrange("(a p) c -> p a c", p=128)), 16)
                        block.gpsimd(f_gp)
                        gw_w = gw.v

                    if hp == 0:
                        norm_hist = []
                        exp_hist = []
                    for hloc in range(2):
                        hbase = 64 * hloc
                        voff = 65 * hloc
                        for t in range(NT):
                            ob = 0 if ((2 * hloc + t) % 2 == 0) else 512
                            se_exp = {}
                            pe_av = {}
                            last_norm = norm_hist[-2] if len(norm_hist) >= 2 else 0
                            last_exp = exp_hist[-1] if exp_hist else 0
                            for sb_ in range(4):
                                def f_ts(te, sb_=sb_, hbase=hbase, t=t, ve_v=ve_v, last_norm=last_norm, last_exp=last_exp):
                                    if sb_ == 0:
                                        te.wait_ge(ve.sem, max(ve_v, last_norm))
                                        te.wait_ge(se.sem, last_exp)
                                    if sb_ >= 1:
                                        te.wait_ge(se.sem, se_exp[sb_ - 1])
                                    for si in range(4):
                                        s = sb_ * 4 + si
                                        i = te.matmul(P1[:, si * TC:(si + 1) * TC],
                                                      kf[hbase:hbase + 64, s * CH:(s + 1) * CH],
                                                      qf[hbase:hbase + 64, t * TC:(t + 1) * TC], start=True, stop=True)
                                    pe.inc(i)
                                    pe_av[('s', sb_)] = pe.v
                                block.tensor(f_ts)

                                def f_se(se_, sb_=sb_):
                                    se_.wait_ge(pe.sem, pe_av[('s', sb_)])
                                    if sb_ >= 1:
                                        se_.wait_ge(pe.sem, pe_av[('a', sb_ - 1)])
                                    se.inc(se_.activation(eb[:], P1[:, :], AF.Exp, scale=0.125))
                                    se_exp[sb_] = se.v
                                block.scalar(f_se)

                                def f_ta(te, sb_=sb_, voff=voff, ob=ob):
                                    te.wait_ge(se.sem, se_exp[sb_])
                                    for si in range(4):
                                        s = sb_ * 4 + si
                                        i = te.matmul(P2[0:65, ob:ob + TC], vt[:, s, voff:voff + 65],
                                                      eb[:, si * TC:(si + 1) * TC],
                                                      start=(sb_ == 0 and si == 0), stop=(sb_ == 3 and si == 3))
                                    pe.inc(i)
                                    pe_av[('a', sb_)] = pe.v
                                block.tensor(f_ta)

                            def f_vn(v, ob=ob):
                                v.wait_ge(pe.sem, pe_av[('a', 3)])
                                ve.inc(v.reciprocal(zrow[:], P2[64:65, ob:ob + TC]))
                            block.vector(f_vn)
                            ve_r = ve.v

                            def f_tz(te, ve_r=ve_r):
                                te.wait_ge(ve.sem, ve_r)
                                pe.inc(te.matmul(P2[0:64, 1024:1024 + TC], ones32[0:1, 0:64],
                                                 zrow[0:1, 0:TC], start=True, stop=True))
                            block.tensor(f_tz)
                            pe_z = pe.v

                            def f_vz(v, hp=hp, hbase=hbase, t=t, ob=ob, pe_z=pe_z):
                                v.wait_ge(pe.sem, pe_z)
                                ve.inc(v.tensor_copy(zb[:], P2[0:64, 1024:1024 + TC]))
                                v.wait_ge(ve.sem, ve.v)
                                ve.inc(v.tensor_tensor(FM[hbase:hbase + 64, hp, t * TC:(t + 1) * TC],
                                                       P2[0:64, ob:ob + TC], zb[:], ALU.mult))
                            block.vector(f_vz)
                            norm_hist.append(ve.v)
                            exp_hist.append(se.v)

                ve_out = {}
                for m in range(G):
                    ve_prev2 = ve.v

                    def f_t8(te, m=m, ve_prev2=ve_prev2):
                        if m < 2:
                            te.wait_ge(ve.sem, ve_prev2)
                        else:
                            te.wait_ge(ve.sem, ve_out[m - 2])
                        PP = P1 if m % 2 == 0 else P2
                        for t in range(NT):
                            for k in range(G):
                                te.matmul(PP[:, t * TC:(t + 1) * TC], owT[:, k, 128 * m:128 * m + 128],
                                          FM[:, k, t * TC:(t + 1) * TC], start=(k == 0), stop=False)
                            i = te.matmul(PP[:, t * TC:(t + 1) * TC], obq[0:1, 128 * m:128 * m + 128],
                                          onesbf[0:1, 0:TC], start=False, stop=True)
                        pe.inc(i)
                    block.tensor(f_t8)
                    pe_out = pe.v

                    def f_v10(v, m=m, pe_out=pe_out):
                        v.wait_ge(pe.sem, pe_out)
                        PP = P1 if m % 2 == 0 else P2
                        ve.inc(v.tensor_tensor(XB[:, m, :], XB[:, m, :], PP[:, :], ALU.add))
                        ve_out[m] = ve.v
                    block.vector(f_v10)

        def emit_ffn():
            s0 = snap()
            with ExitStack() as fctx:
                phase_id[0] += 1
                fsb = lambda n, s, d: fctx.enter_context(nc.sbuf_tensor(n + str(phase_id[0]), s, d))
                wub = fsb("wub", [128, 8, SLAB], BF16)
                wgb = fsb("wgb", [128, 8, SLAB], BF16)
                wdb = fsb("wdb", [128, 4, 1024], BF16)
                LG = fsb("LG", [128, 128], F32)
                SBB = fsb("SBB", [128, 128], F32)
                et = fsb("et", [128, 128], F32)
                mk = fsb("mk", [128, 128], F32)
                pa = fsb("pa", [128, 128], F32)
                pb = fsb("pb", [128, 128], F32)
                ctm = fsb("ctm", [128, 128], F32)
                lgt = fsb("lgt", [128, 128], F32)
                abt_a = fsb("abt_a", [128, 16], F32)
                abt_b = fsb("abt_b", [128, 16], F32)
                mx16 = fsb("mx16", [128, 16], F32)
                z16 = fsb("z16", [128, 16], F32)
                c16 = fsb("c16", [128, 16], F32)
                cbrow = fsb("cbrow", [1, T], F32)
                scrB = fsb("scrB", [128, T], F32)

                def f_t1(te):
                    wsnap(te, s0)
                    for t in range(NT):
                        for k in range(G):
                            i = te.matmul(P1[0:8, t * TC:(t + 1) * TC], wr32[:, k, :],
                                          XB[:, k, t * TC:(t + 1) * TC], start=(k == 0), stop=(k == G - 1))
                        pe.inc(i)
                    for k in range(G):
                        i = te.matmul(P2[0:1, 0:8], onescol32[:], wr32[:, k, :], start=(k == 0), stop=(k == G - 1))
                    pe.inc(i)
                block.tensor(f_t1)
                pe_raw = pe.v

                def f_v1(v):
                    v.wait_ge(pe.sem, pe_raw)
                    ve.inc(v.tensor_copy(scrA[0:8, :], P1[0:8, :]))
                    ve.inc(v.tensor_copy(wrsrow[:], P2[0:1, 0:8]))
                block.vector(f_v1)
                ve_raw = ve.v

                def f_t2(te):
                    te.wait_ge(ve.sem, ve_raw)
                    for tt in range(NCH):
                        pe.inc(te.transpose(P2[:, tt * 8:(tt + 1) * 8], scrA[0:8, tt * CH:(tt + 1) * CH], id32[0:8, 0:8]))
                    for tt in range(NCH):
                        te.transpose(P2[:, 256 + tt:256 + tt + 1], arow[0:1, tt * CH:(tt + 1) * CH], id32[0:1, 0:1])
                        pe.inc(te.transpose(P2[:, 288 + tt:288 + tt + 1], brow[0:1, tt * CH:(tt + 1) * CH], id32[0:1, 0:1]))
                    pe.inc(te.matmul(P1[:, 1024:1032], ones32[0:1, 0:128], wrsrow[0:1, 0:8], start=True, stop=True))
                block.tensor(f_t2)
                pe_tp = pe.v

                def f_v2(v):
                    v.wait_ge(pe.sem, pe_tp)
                    v.tensor_copy(lgt[:], P2[:, 0:128])
                    v.tensor_copy(abt_a[:], P2[:, 256:272])
                    v.tensor_copy(abt_b[:], P2[:, 288:304])
                    ve.inc(v.tensor_copy(wrsbc[:], P1[:, 1024:1032]))
                    v.wait_ge(ve.sem, ve.v)
                    lg3 = LG.ap().rearrange("p (t e) -> p t e", e=8)
                    raw3 = lgt.ap().rearrange("p (t e) -> p t e", e=8)
                    a3 = abt_a.ap().rearrange("p (t o) -> p t o", o=1)
                    b3 = abt_b.ap().rearrange("p (t o) -> p t o", o=1)
                    for jj in range(8):
                        v.tensor_tensor(lg3[:, :, jj:jj + 1], raw3[:, :, jj:jj + 1], a3, ALU.mult)
                    ve.inc(v.engine_nop())
                    v.wait_ge(ve.sem, ve.v)
                    for jj in range(8):
                        v.scalar_tensor_tensor(lg3[:, :, jj:jj + 1], b3, wrsbc[:, jj:jj + 1],
                                               lg3[:, :, jj:jj + 1], ALU.mult, ALU.add)
                    ve.inc(v.engine_nop())
                    v.wait_ge(ve.sem, ve.v)
                    mx3 = mx16.ap().rearrange("p (t o) -> p t o", o=1)
                    ve.inc(v.reduce_max(mx3, lg3, axis=AX))
                    v.wait_ge(ve.sem, ve.v)
                    sb3 = SBB.ap().rearrange("p (t e) -> p t e", e=8)
                    for jj in range(8):
                        v.tensor_tensor(sb3[:, :, jj:jj + 1], lg3[:, :, jj:jj + 1], mx3, ALU.subtract)
                    ve.inc(v.engine_nop())
                block.vector(f_v2)
                ve_sub = ve.v

                def f_s1(se_):
                    se_.wait_ge(ve.sem, ve_sub)
                    se.inc(se_.activation(et[:], SBB[:], AF.Exp))
                block.scalar(f_s1)
                se_expv = se.v

                def f_v3(v):
                    v.wait_ge(se.sem, se_expv)
                    et3 = et.ap().rearrange("p (t e) -> p t e", e=8)
                    z3 = z16.ap().rearrange("p (t o) -> p t o", o=1)
                    ve.inc(v.reduce_sum(z3, et3, axis=AX))
                    v.wait_ge(ve.sem, ve.v)
                    ve.inc(v.reciprocal(z16[:], z16[:]))
                    v.wait_ge(ve.sem, ve.v)
                    ve.inc(v.tensor_scalar(c16[:], z16[:], 2.0, None, ALU.mult))
                    ve.inc(v.tensor_scalar(mk[:], et[:], 1.0, None, ALU.is_equal))
                    v.wait_ge(ve.sem, ve.v)
                    mk3 = mk.ap().rearrange("p (t e) -> p t e", e=8)
                    pa3 = pa.ap().rearrange("p (t e) -> p t e", e=8)
                    pb3 = pb.ap().rearrange("p (t e) -> p t e", e=8)
                    v.tensor_copy(pa3[:, :, 0:1], mk3[:, :, 0:1])
                    ve.inc(v.tensor_tensor(pa3[:, :, 1:8], mk3[:, :, 1:8], mk3[:, :, 0:7], ALU.add))
                    v.wait_ge(ve.sem, ve.v)
                    v.tensor_copy(pb3[:, :, 0:2], pa3[:, :, 0:2])
                    ve.inc(v.tensor_tensor(pb3[:, :, 2:8], pa3[:, :, 2:8], pa3[:, :, 0:6], ALU.add))
                    v.wait_ge(ve.sem, ve.v)
                    v.tensor_copy(pa3[:, :, 0:4], pb3[:, :, 0:4])
                    ve.inc(v.tensor_tensor(pa3[:, :, 4:8], pb3[:, :, 4:8], pb3[:, :, 0:4], ALU.add))
                    v.wait_ge(ve.sem, ve.v)
                    ve.inc(v.tensor_scalar(pb[:], pa[:], 1.0, None, ALU.is_equal))
                    v.wait_ge(ve.sem, ve.v)
                    ve.inc(v.tensor_tensor(mk[:], mk[:], pb[:], ALU.mult))
                    v.wait_ge(ve.sem, ve.v)
                    ctm3 = ctm.ap().rearrange("p (t e) -> p t e", e=8)
                    c3 = c16.ap().rearrange("p (t o) -> p t o", o=1)
                    for jj in range(8):
                        v.tensor_tensor(ctm3[:, :, jj:jj + 1], mk3[:, :, jj:jj + 1], c3, ALU.mult)
                    ve.inc(v.engine_nop())
                block.vector(f_v3)
                ve_ctm = ve.v

                def f_t3(te):
                    te.wait_ge(ve.sem, ve_ctm)
                    for tt in range(NCH):
                        pe.inc(te.transpose(P1[0:8, tt * CH:(tt + 1) * CH], ctm[:, tt * 8:(tt + 1) * 8], id32[:]))
                block.tensor(f_t3)
                pe_ct = pe.v

                def f_v4(v):
                    v.wait_ge(pe.sem, pe_ct)
                    ve.inc(v.tensor_copy(scrA[0:8, :], P1[0:8, :]))
                block.vector(f_v4)
                ve_crows = ve.v

                for e in range(8):
                    shared = (e == 7)
                    if not shared:
                        pe_now = pe.v

                        def f_sy(sy, e=e, pe_now=pe_now):
                            sy.wait_ge(ve.sem, ve_crows)
                            sy.wait_ge(pe.sem, pe_now)
                            ld.inc(sy.dma_start(out=cbrow[:], in_=scrA[e:e + 1, :]), 16)
                        block.sync(f_sy)
                        ld_cb = ld.v
                        ve_pp2 = ve.v

                        def f_tc(te, ld_cb=ld_cb, ve_pp2=ve_pp2):
                            te.wait_ge(ld.sem, ld_cb)
                            te.wait_ge(ve.sem, ve_pp2)
                            for t in range(NT):
                                i = te.matmul(P1[:, t * TC:(t + 1) * TC], ones32[0:1, 0:128],
                                              cbrow[0:1, t * TC:(t + 1) * TC], start=True, stop=True)
                            pe.inc(i)
                        block.tensor(f_tc)
                        pe_cb = pe.v

                        def f_vc(v, pe_cb=pe_cb):
                            v.wait_ge(pe.sem, pe_cb)
                            ve.inc(v.tensor_copy(scrB[:], P1[:, :]))
                        block.vector(f_vc)
                    ve_cbc = ve.v

                    def f_gw0(g_, e=e, shared=shared):
                        g_.wait_ge(pe.sem, pe.v)
                        if shared:
                            gw.inc(g_.dma_start(out=wub[:], in_=wus_d[:, 0:SLAB].rearrange("(k p) c -> p k c", p=128)), 16)
                            gw.inc(g_.dma_start(out=wgb[:], in_=wgs_d[:, 0:SLAB].rearrange("(k p) c -> p k c", p=128)), 16)
                            gw.inc(g_.dma_start(out=wdb[:], in_=wds_d[0:SLAB, :].rearrange("(k p) c -> p k c", p=128)), 16)
                        else:
                            gw.inc(g_.dma_start(out=wub[:], in_=wu_d[e, :, 0:SLAB].rearrange("(k p) c -> p k c", p=128)), 16)
                            gw.inc(g_.dma_start(out=wgb[:], in_=wg_d[e, :, 0:SLAB].rearrange("(k p) c -> p k c", p=128)), 16)
                            gw.inc(g_.dma_start(out=wdb[:], in_=wd_d[e, 0:SLAB, :].rearrange("(k p) c -> p k c", p=128)), 16)
                    block.gpsimd(f_gw0)

                    for s in range(NSLAB):
                        gw_u, gw_g, gw_d = gw.v - 32, gw.v - 16, gw.v

                        ve_up = {}
                        for m in range(4):
                            def f_tu(te, m=m, s=s, gw_u=gw_u, ve_cbc=ve_cbc):
                                if m == 0:
                                    te.wait_ge(gw.sem, gw_u)
                                    if s == 0:
                                        te.wait_ge(ve.sem, ve_cbc)
                                if m >= 2:
                                    te.wait_ge(ve.sem, ve_up[m - 2])
                                PP = P1 if m % 2 == 0 else P2
                                for t in range(NT):
                                    for k in range(G):
                                        i = te.matmul(PP[:, t * TC:(t + 1) * TC], wub[:, k, m * 128:(m + 1) * 128],
                                                      XN[:, k, t * TC:(t + 1) * TC], start=(k == 0), stop=(k == G - 1))
                                    pe.inc(i)
                            block.tensor(f_tu)
                            pe_um = pe.v

                            def f_vu(v, m=m, pe_um=pe_um, shared=shared):
                                v.wait_ge(pe.sem, pe_um)
                                PP = P1 if m % 2 == 0 else P2
                                if shared:
                                    ve.inc(v.tensor_copy(FM[:, m, :], PP[:, :]))
                                else:
                                    ve.inc(v.tensor_tensor(FM[:, m, :], PP[:, :], scrB[:], ALU.mult))
                                ve_up[m] = ve.v
                            block.vector(f_vu)

                        se_gt = {}
                        for m in range(4):
                            def f_tg(te, m=m, gw_g=gw_g):
                                if m == 0:
                                    te.wait_ge(gw.sem, gw_g)
                                te.wait_ge(ve.sem, ve_up[min(m + 2, 3)])
                                if m >= 2:
                                    te.wait_ge(se.sem, se_gt[m - 2])
                                PP = P1 if m % 2 == 0 else P2
                                for t in range(NT):
                                    for k in range(G):
                                        i = te.matmul(PP[:, t * TC:(t + 1) * TC], wgb[:, k, m * 128:(m + 1) * 128],
                                                      XN[:, k, t * TC:(t + 1) * TC], start=(k == 0), stop=(k == G - 1))
                                    pe.inc(i)
                            block.tensor(f_tg)
                            pe_gm = pe.v

                            def f_sg(se_, m=m, pe_gm=pe_gm):
                                se_.wait_ge(pe.sem, pe_gm)
                                se.inc(se_.activation(FM[:, 4 + m, :], (P1 if m % 2 == 0 else P2)[:, :], AF.Silu))
                                se_gt[m] = se.v
                            block.scalar(f_sg)
                            se_now2 = se.v

                            def f_vh(v, m=m, se_now2=se_now2):
                                v.wait_ge(se.sem, se_now2)
                                ve.inc(v.tensor_tensor(FM[:, m, :], FM[:, m, :], FM[:, 4 + m, :], ALU.mult))
                            block.vector(f_vh)
                        ve_hid = ve.v

                        if DBG_DUMP == "hid0" and e == 0 and s == 0:
                            def f_vdbg2(v):
                                v.wait_ge(ve.sem, ve_hid)
                                ve.inc(v.tensor_copy(scrA[:], FM[:, 0, :]))
                            block.vector(f_vdbg2)

                        pe_gt_done = pe.v

                        def f_gwn(g_, e=e, s=s, shared=shared, pe_gt_done=pe_gt_done):
                            if s + 1 < NSLAB:
                                g_.wait_ge(pe.sem, pe_gt_done)
                                c0 = (s + 1) * SLAB
                                if shared:
                                    gw.inc(g_.dma_start(out=wub[:], in_=wus_d[:, c0:c0 + SLAB].rearrange("(k p) c -> p k c", p=128)), 16)
                                    gw.inc(g_.dma_start(out=wgb[:], in_=wgs_d[:, c0:c0 + SLAB].rearrange("(k p) c -> p k c", p=128)), 16)
                                else:
                                    gw.inc(g_.dma_start(out=wub[:], in_=wu_d[e, :, c0:c0 + SLAB].rearrange("(k p) c -> p k c", p=128)), 16)
                                    gw.inc(g_.dma_start(out=wgb[:], in_=wg_d[e, :, c0:c0 + SLAB].rearrange("(k p) c -> p k c", p=128)), 16)
                        block.gpsimd(f_gwn)

                        ve_dn = {}
                        for mo in range(G):
                            def f_td(te, mo=mo, gw_d=gw_d, ve_hid=ve_hid):
                                if mo == 0:
                                    te.wait_ge(gw.sem, gw_d)
                                    te.wait_ge(ve.sem, ve_hid)
                                if mo >= 2:
                                    te.wait_ge(ve.sem, ve_dn[mo - 2])
                                PP = P1 if mo % 2 == 0 else P2
                                for t in range(NT):
                                    for k in range(4):
                                        i = te.matmul(PP[:, t * TC:(t + 1) * TC], wdb[:, k, mo * 128:(mo + 1) * 128],
                                                      FM[:, k, t * TC:(t + 1) * TC], start=(k == 0), stop=(k == 3))
                                    pe.inc(i)
                            block.tensor(f_td)
                            pe_dm = pe.v

                            def f_vd(v, mo=mo, pe_dm=pe_dm):
                                v.wait_ge(pe.sem, pe_dm)
                                PP = P1 if mo % 2 == 0 else P2
                                ve.inc(v.tensor_tensor(XB[:, mo, :], XB[:, mo, :], PP[:, :], ALU.add))
                                ve_dn[mo] = ve.v
                            block.vector(f_vd)

                        pe_dn_done = pe.v

                        if DBG_DUMP == "cbc" and e == 6 and s == 0:
                            def f_vdbg(v):
                                v.wait_ge(ve.sem, ve.v)
                                ve.inc(v.tensor_copy(scrA[:], scrB[:]))
                            block.vector(f_vdbg)

                        def f_gwd(g_, e=e, s=s, shared=shared, pe_dn_done=pe_dn_done):
                            if s + 1 < NSLAB:
                                g_.wait_ge(pe.sem, pe_dn_done)
                                c0 = (s + 1) * SLAB
                                if shared:
                                    gw.inc(g_.dma_start(out=wdb[:], in_=wds_d[c0:c0 + SLAB, :].rearrange("(k p) c -> p k c", p=128)), 16)
                                else:
                                    gw.inc(g_.dma_start(out=wdb[:], in_=wd_d[e, c0:c0 + SLAB, :].rearrange("(k p) c -> p k c", p=128)), 16)
                        block.gpsimd(f_gwd)

        def emit_store():
            s0 = snap()
            if DBG_DUMP == "abrows":
                def f_sab(sy):
                    wsnap(sy, s0)
                    st.inc(sy.dma_start(out=y_d[0:1, :], in_=arow[:]), 16)
                    st.inc(sy.dma_start(out=y_d[1:2, :], in_=brow[:]), 16)
                    sy.wait_ge(st.sem, st.v)
                block.sync(f_sab)
                return
            if DBG_DUMP in ("scrA", "pkv", "cbc", "hid0"):
                def f_sd(sy):
                    wsnap(sy, s0)
                    st.inc(sy.dma_start(out=y_d[0:128, :], in_=scrA[:]), 16)
                    sy.wait_ge(st.sem, st.v)
                block.sync(f_sd)
                return
            if DBG_DUMP is not None:
                buf = {"FM": FM, "XN": XN}[DBG_DUMP]
                for g in range(G):
                    stv = st.v

                    def f_vd(v, g=g, stv=stv):
                        if g == 0:
                            wsnap(v, s0)
                        v.wait_ge(st.sem, stv)
                        ve.inc(v.tensor_copy(scrA[:], buf[:, g, :]))
                    block.vector(f_vd)
                    vev = ve.v

                    def f_sd(sy, g=g, vev=vev):
                        sy.wait_ge(ve.sem, vev)
                        st.inc(sy.dma_start(out=y_d[g * 128:(g + 1) * 128, :], in_=scrA[:]), 16)
                    block.sync(f_sd)

                def f_fin(sy):
                    sy.wait_ge(st.sem, st.v)
                block.sync(f_fin)
                return
            if DBG_XN:
                def f_vx(v):
                    wsnap(v, s0)
                    for g in range(G):
                        i = v.tensor_copy(scrA[:], XN[:, g, :])
                        ve.inc(i)
                block.vector(f_vx)

            def f_sy(sy):
                wsnap(sy, s0)
                if DBG_XN:
                    sy.wait_ge(ve.sem, ve.v)
                for g in range(G):
                    st.inc(sy.dma_start(out=y_d[g * 128:(g + 1) * 128, :], in_=XB[:, g, :]), 16)
                sy.wait_ge(st.sem, st.v)
            block.sync(f_sy)

        if nsub >= 1:
            emit_ln()
            if DBG_STOP != "ln":
                emit_attn_a()
        if nsub >= 2:
            emit_ln()
            emit_ffn()
        if nsub >= 3:
            emit_ln()
            emit_attn_b()
        if nsub >= 4:
            emit_ln()
            emit_ffn()
        emit_store()

    return nc


def kernel(**inputs):
    cst = host_consts()
    xs = np.ascontiguousarray(np.transpose(np.asarray(inputs["L0_attn_a_x"], np.float32), (0, 2, 1)))
    pkvT1 = np.asarray(inputs["L0_attn_a_past_kv"], np.float32).transpose(1, 0, 2).reshape(64, 1024)
    pkvT = np.ascontiguousarray(np.concatenate([pkvT1, pkvT1], axis=0))
    base = dict(
        wq=np.ascontiguousarray(inputs["L0_attn_a_Wq"], np.float32),
        wk=np.ascontiguousarray(inputs["L0_attn_a_Wk"], np.float32),
        wv=np.ascontiguousarray(inputs["L0_attn_a_Wv"], np.float32),
        pkvT=pkvT,
        gnw=np.ascontiguousarray(np.asarray(inputs["L0_attn_a_gn_w"], np.float32).reshape(H, 1)),
        gnb=np.ascontiguousarray(np.asarray(inputs["L0_attn_a_gn_b"], np.float32).reshape(H, 1)),
        wwT=np.ascontiguousarray(np.asarray(inputs["L0_attn_b_W_w"], np.float32).T),
        wb=np.ascontiguousarray(np.asarray(inputs["L0_attn_b_W_b"], np.float32).reshape(1, 3072)),
        owT=np.ascontiguousarray(np.asarray(inputs["L0_attn_b_out_w"], np.float32).T),
        ob=np.ascontiguousarray(np.asarray(inputs["L0_attn_b_out_b"], np.float32).reshape(1, 1024)),
        wr=np.ascontiguousarray(inputs["L0_ffn_Wr"], np.float32),
        wu=np.ascontiguousarray(inputs["L0_ffn_Wu"], np.float32),
        wg=np.ascontiguousarray(inputs["L0_ffn_Wg"], np.float32),
        wd=np.ascontiguousarray(inputs["L0_ffn_Wd"], np.float32),
        wus=np.ascontiguousarray(inputs["L0_ffn_Wu_s"], np.float32),
        wgs=np.ascontiguousarray(inputs["L0_ffn_Wg_s"], np.float32),
        wds=np.ascontiguousarray(inputs["L0_ffn_Wd_s"], np.float32),
        dmask=cst["dmask"], dmaskT=cst["dmaskT"], gq=cst["gq"], gc=cst["gc"],
        gk=cst["gk"], gcu=cst["gcu"], sel=cst["sel"], ident=cst["ident"], ones32=cst["ones32"],
    )
    nc = build_nc(NSUB)
    in_maps = [dict(base, x=np.ascontiguousarray(xs[b])) for b in range(8)]
    res = run_bass_kernel_spmd(nc, in_maps, core_ids=list(range(8)))
    out = np.stack([np.asarray(r["y"], np.float32).T for r in res.results], axis=0)
    return out
